# version 01
# baseline (speedup 1.0000x reference)
import contextlib
import numpy as np
import concourse.bass as bass
import concourse.mybir as mybir
from concourse.bass_utils import run_bass_kernel_spmd

F32 = mybir.dt.float32
BF16 = mybir.dt.bfloat16
AF = mybir.ActivationFunctionType
ALU = mybir.AluOpType
AX = mybir.AxisListType

D = 1024
TCORE = 4096
TT = 512
NT = TCORE // TT
DC = D // 128
NCORES = 8
LN_EPS = 1e-5


def slab_major(wT, tile_o=128):
    K, O = wT.shape
    kc, ot = K // 128, O // tile_o
    a = wT.reshape(kc, 128, ot, tile_o).transpose(2, 1, 0, 3)
    return np.ascontiguousarray(a.reshape(ot, 128, kc * tile_o), dtype=np.float32)


def col_major_vec(v):
    O = v.shape[0]
    return np.ascontiguousarray(v.reshape(O // 128, 128).T, dtype=np.float32)


class Prog:

    def __init__(self, nc):
        self.nc = nc
        self.ops = {"sync": [], "gpsimd": [], "tensor": [], "vector": [], "scalar": []}

    def emit(self, eng, fn):
        self.ops[eng].append(fn)

    def play(self, block):
        ops = self.ops

        @block.sync
        def _(e):
            for f in ops["sync"]:
                f(e)

        @block.gpsimd
        def _(e):
            for f in ops["gpsimd"]:
                f(e)

        @block.tensor
        def _(e):
            for f in ops["tensor"]:
                f(e)

        @block.vector
        def _(e):
            for f in ops["vector"]:
                f(e)

        @block.scalar
        def _(e):
            for f in ops["scalar"]:
                f(e)


class Sem:
    def __init__(self, handle):
        self.h = handle
        self.v = 0

    def bump(self, k=1):
        self.v += k
        return self.v


def mk_sems(stack, nc, names):
    return {n: Sem(stack.enter_context(nc.semaphore(n))) for n in names}


def emit_ln(p, s, bufs, i, xn_dtype=BF16):
    nc = p.nc
    (xbuf, xnb, xc, x2c, stat_rhs, onesb, tmp,
     mrow, vrow, m2row, lnrow, rrow,
     psA, psB, psBCA, psBCB) = (
        bufs["xbuf"], bufs["xnb"], bufs["xc"], bufs["x2c"], bufs["stat_rhs"],
        bufs["onesb"], bufs["tmp"], bufs["mrow"], bufs["vrow"], bufs["m2row"],
        bufs["lnrow"], bufs["rrow"], bufs["psA"], bufs["psB"],
        bufs["psBCA"], bufs["psBCB"])
    apx = lambda t: t if isinstance(t, bass.AP) else t[:]
    psA, psB, psBCA, psBCB = apx(psA), apx(psB), apx(psBCA), apx(psBCB)
    par = i % bufs.get("x_bufs", 2)
    xpar = i % bufs["xn_bufs"]

    for c in range(DC):
        C = i * DC + c
        cb = c % 2
        if C >= 2:
            tgt = C - 1
            p.emit("scalar", lambda e, t=tgt: e.wait_ge(s["SMM"].h, t))
            p.emit("vector", lambda e, t=tgt: e.wait_ge(s["SMM"].h, t))
        if c == 0:
            tgt = 16 * (i + 1)
            p.emit("scalar", lambda e, t=tgt: e.wait_ge(s["X"].h, t))
            p.emit("vector", lambda e, t=tgt: e.wait_ge(s["X"].h, t))
        p.emit("scalar", lambda e, par=par, c=c, cb=cb: e.activation(
            x2c[:, cb, :], xbuf[:, par, c, :], AF.Square).then_inc(s["CAST"].h, 1))
        s["CAST"].bump()
        p.emit("vector", lambda e, par=par, c=c, cb=cb: e.tensor_copy(
            xc[:, cb, :], xbuf[:, par, c, :]).then_inc(s["CASV"].h, 1))
        s["CASV"].bump()
        p.emit("tensor", lambda e, t=s["CAST"].v: e.wait_ge(s["CAST"].h, t))
        p.emit("tensor", lambda e, t=s["CASV"].v: e.wait_ge(s["CASV"].h, t))
        if c == 0:
            if i == 0:
                p.emit("tensor", lambda e, t=s["LD2"].v: e.wait_ge(s["LD2"].h, t))
            else:
                tgt2 = i
                p.emit("tensor", lambda e, t=tgt2: e.wait_ge(s["STV"].h, t))
        p.emit("tensor", lambda e, cb=cb, c=c: e.matmul(
            psA, onesb[:, 0:1], xc[:, cb, :], start=(c == 0), stop=(c == DC - 1)))
        p.emit("tensor", lambda e, cb=cb, c=c: e.matmul(
            psB, onesb[:, 0:1], x2c[:, cb, :],
            start=(c == 0), stop=(c == DC - 1)).then_inc(s["SMM"].h, 1))
        s["SMM"].bump()
    tgt = s["SMM"].v
    p.emit("vector", lambda e, t=tgt: e.wait_ge(s["SMM"].h, t))
    p.emit("vector", lambda e: e.tensor_scalar(mrow[:], psA, 1.0 / D, None, ALU.mult))
    p.emit("vector", lambda e: e.tensor_scalar(
        vrow[:], psB, 1.0 / D, LN_EPS, ALU.mult, ALU.add))
    p.emit("vector", lambda e: e.tensor_tensor(m2row[:], mrow[:], mrow[:], ALU.mult))
    p.emit("vector", lambda e: e.tensor_tensor(
        vrow[:], vrow[:], m2row[:], ALU.subtract).then_inc(s["STV"].h, 1))
    s["STV"].bump()
    tgt = s["STV"].v
    p.emit("scalar", lambda e, t=tgt: e.wait_ge(s["STV"].h, t))
    p.emit("scalar", lambda e: e.activation(lnrow[:], vrow[:], AF.Ln))
    p.emit("scalar", lambda e: e.activation(
        rrow[:], lnrow[:], AF.Exp, scale=bufs["neghalf"][0:1, :]))
    p.emit("scalar", lambda e: e.copy(
        stat_rhs[0:1, 0:TT], rrow[:]).then_inc(s["STSC"].h, 1))
    s["STSC"].bump()
    tgt = s["STSC"].v
    p.emit("vector", lambda e, t=tgt: e.wait_ge(s["STSC"].h, t))
    p.emit("vector", lambda e: e.scalar_tensor_tensor(
        stat_rhs[0:1, TT:2 * TT], mrow[:], -1.0, rrow[:],
        ALU.mult, ALU.mult).then_inc(s["AB"].h, 1))
    s["AB"].bump()
    tgt = s["AB"].v
    p.emit("tensor", lambda e, t=tgt: e.wait_ge(s["AB"].h, t))
    if i > 0:
        tgt2 = i
        p.emit("tensor", lambda e, t=tgt2: e.wait_ge(s["XN"].h, t))
    p.emit("tensor", lambda e: e.matmul(
        psBCA, onesb[0:32, :], stat_rhs[:, 0:TT], start=True, stop=True))
    p.emit("tensor", lambda e: e.matmul(
        psBCB, onesb[0:32, :], stat_rhs[:, TT:2 * TT],
        start=True, stop=True).then_inc(s["BC"].h, 1))
    s["BC"].bump()
    tgt = s["BC"].v
    p.emit("vector", lambda e, t=tgt: e.wait_ge(s["BC"].h, t))
    for c in range(DC):
        p.emit("vector", lambda e, par=par, c=c: e.tensor_tensor(
            tmp[:], xbuf[:, par, c, :], psBCA, ALU.mult))
        if c < DC - 1:
            p.emit("vector", lambda e, xp=xpar, c=c: e.tensor_tensor(
                xnb[:, xp, c, :], tmp[:], psBCB, ALU.add))
        else:
            p.emit("vector", lambda e, xp=xpar, c=c: e.tensor_tensor(
                xnb[:, xp, c, :], tmp[:], psBCB, ALU.add).then_inc(s["XN"].h, 1))
    s["XN"].bump()


def emit_x_io(p, s, x_in, x_out, xbuf, i, epi_ready, nt=NT):
    par = i % 2
    src = x_in.rearrange("(dc p) t -> p dc t", p=128)
    dst = x_out.rearrange("(dc p) t -> p dc t", p=128)
    if i == 0:
        p.emit("sync", lambda e: e.dma_start(
            out=xbuf[:, 0, :, :], in_=src[:, :, 0:TT]).then_inc(s["X"].h, 16))
        s["X"].bump(16)
        p.emit("sync", lambda e: e.dma_start(
            out=xbuf[:, 1, :, :], in_=src[:, :, TT:2 * TT]).then_inc(s["X"].h, 16))
        s["X"].bump(16)
    for sem, tgt in epi_ready:
        p.emit("sync", lambda e, sem=sem, t=tgt: e.wait_ge(sem.h, t))
    p.emit("sync", lambda e, par=par, i=i: e.dma_start(
        out=dst[:, :, i * TT:(i + 1) * TT], in_=xbuf[:, par, :, :]).then_inc(s["ST"].h, 16))
    s["ST"].bump(16)
    if i + 2 < nt:
        tgt = s["ST"].v
        p.emit("sync", lambda e, t=tgt: e.wait_ge(s["ST"].h, t))
        p.emit("sync", lambda e, par=par, i=i: e.dma_start(
            out=xbuf[:, par, :, :],
            in_=src[:, :, (i + 2) * TT:(i + 3) * TT]).then_inc(s["X"].h, 16))
        s["X"].bump(16)


def ln_bufs(stack, nc, xn_dtype=BF16, xn_bufs=2):
    sb, ps = nc.sbuf_tensor, nc.psum_tensor
    b = {}
    b["xn_bufs"] = xn_bufs
    b["xbuf"] = stack.enter_context(sb("xbuf", [128, 2, DC, TT], F32))
    b["xnb"] = stack.enter_context(sb("xnb", [128, xn_bufs, DC, TT], xn_dtype))
    b["xc"] = stack.enter_context(sb("xc", [128, 2, TT], BF16))
    b["x2c"] = stack.enter_context(sb("x2c", [128, 2, TT], BF16))
    b["stat_rhs"] = stack.enter_context(sb("stat_rhs", [32, 2 * TT], BF16))
    b["onesb"] = stack.enter_context(sb("onesb", [128, 128], BF16))
    b["tmp"] = stack.enter_context(sb("tmp", [128, TT], F32))
    for nm in ("mrow", "vrow", "m2row", "lnrow", "rrow"):
        b[nm] = stack.enter_context(sb(nm, [1, TT], F32))
    b["psA"] = stack.enter_context(ps("psA", [1, TT], F32))
    b["psB"] = stack.enter_context(ps("psB", [1, TT], F32))
    b["psBCA"] = stack.enter_context(ps("psBCA", [128, TT], F32))
    b["psBCB"] = stack.enter_context(ps("psBCB", [128, TT], F32))
    b["neghalf"] = stack.enter_context(sb("neghalf", [128, 1], F32))
    return b


LN_SEMS = ["X", "ST", "SMM", "CAST", "CASV", "STV", "STSC", "AB", "BC", "XN", "LD1", "LD2"]


def emit_fence(p, eng, fb, sem=None, k=1):
    n = fb.shape[-1] // 2
    if eng == "vector":
        if sem is None:
            p.emit("vector", lambda e: e.tensor_copy(fb[:, n:2 * n], fb[:, 0:n]))
        else:
            p.emit("vector", lambda e: e.tensor_copy(
                fb[:, n:2 * n], fb[:, 0:n]).then_inc(sem.h, k))
    elif eng == "scalar":
        if sem is None:
            p.emit("scalar", lambda e: e.activation(
                fb[:, n:2 * n], fb[:, 0:n], AF.Identity))
        else:
            p.emit("scalar", lambda e: e.activation(
                fb[:, n:2 * n], fb[:, 0:n], AF.Identity).then_inc(sem.h, k))
    elif eng == "gpsimd":
        if sem is None:
            p.emit("gpsimd", lambda e: e.tensor_copy(fb[:, n:2 * n], fb[:, 0:n]))
        else:
            p.emit("gpsimd", lambda e: e.tensor_copy(
                fb[:, n:2 * n], fb[:, 0:n]).then_inc(sem.h, k))
    if sem is not None:
        sem.bump(k)


def emit_dma_drain(p, s):
    for nm, eng in (("X", "sync"), ("ST", "sync"), ("LD1", "sync"),
                    ("WL", "gpsimd"), ("LD2", "gpsimd")):
        if nm in s and s[nm].v > 0:
            p.emit(eng, lambda e, h=s[nm].h, t=s[nm].v: e.wait_ge(h, t))


def emit_onetime(p, s, bufs, scale_loads, gp_loads):
    for dst, src in scale_loads:
        p.emit("sync", lambda e, d=dst, sr=src: e.dma_start(out=d, in_=sr).then_inc(s["LD1"].h, 16))
        s["LD1"].bump(16)
    for dst, src in gp_loads:
        p.emit("gpsimd", lambda e, d=dst, sr=src: e.dma_start(out=d, in_=sr).then_inc(s["LD2"].h, 16))
        s["LD2"].bump(16)
    p.emit("vector", lambda e: e.memset(bufs["stat_rhs"][:], 0.0))
    p.emit("vector", lambda e: e.memset(bufs["neghalf"][:], -0.5))


def build_ternary(nt=NT):
    nc = bass.Bass("TRN2", target_bir_lowering=False, detect_race_conditions=False)
    x_in = nc.dram_tensor("x", [D, TCORE], F32, kind="ExternalInput").ap()
    x_out = nc.dram_tensor("xo", [D, TCORE], F32, kind="ExternalOutput").ap()
    w1s = nc.dram_tensor("w1s", [32, 128, 1024], F32, kind="ExternalInput").ap()
    w2s = nc.dram_tensor("w2s", [32, 128, 1024], F32, kind="ExternalInput").ap()
    h1s = nc.dram_tensor("h1s", [32, 128, 4096], F32, kind="ExternalInput").ap()
    h2s = nc.dram_tensor("h2s", [32, 128, 4096], F32, kind="ExternalInput").ap()
    w3s = nc.dram_tensor("w3s", [8, 128, 4096], F32, kind="ExternalInput").ap()
    s1v = nc.dram_tensor("s1v", [128, 32], F32, kind="ExternalInput").ap()
    s2v = nc.dram_tensor("s2v", [128, 32], F32, kind="ExternalInput").ap()
    sh1v = nc.dram_tensor("sh1v", [128, 32], F32, kind="ExternalInput").ap()
    sh2v = nc.dram_tensor("sh2v", [128, 32], F32, kind="ExternalInput").ap()
    s3v = nc.dram_tensor("s3v", [128, 8], F32, kind="ExternalInput").ap()
    ones_in = nc.dram_tensor("ones_in", [128, 128], F32, kind="ExternalInput").ap()

    NSLAB = 4
    with contextlib.ExitStack() as stack:
        b = ln_bufs(stack, nc)
        sb = nc.sbuf_tensor
        h0 = stack.enter_context(sb("h0", [128, 32, TT], BF16))
        hg = stack.enter_context(sb("hg", [128, 32, TT], BF16))
        wsl = stack.enter_context(sb("wsl", [128, NSLAB, 4096], BF16))
        g_s = stack.enter_context(sb("g_s", [128, 2, TT], F32))
        u_s = stack.enter_context(sb("u_s", [128, 2, TT], BF16))
        s1sb = stack.enter_context(sb("s1sb", [128, 32], F32))
        s2sb = stack.enter_context(sb("s2sb", [128, 32], F32))
        sh1sb = stack.enter_context(sb("sh1sb", [128, 32], F32))
        sh2sb = stack.enter_context(sb("sh2sb", [128, 32], F32))
        s3sb = stack.enter_context(sb("s3sb", [128, 8], F32))
        s3h = stack.enter_context(sb("s3h", [128, 8], F32))
        psm = [stack.enter_context(nc.psum_tensor(f"psm{k}", [128, TT], F32))
               for k in range(4)]
        s = mk_sems(stack, nc, LN_SEMS + ["WL", "GD", "H0", "ONE",
                                          "EP0", "EP1", "EP2", "EP3"])
        EP = [s["EP0"], s["EP1"], s["EP2"], s["EP3"]]
        bank_uses = [0, 0, 0, 0]

        with nc.Block() as block:
            p = Prog(nc)
            emit_onetime(p, s, b, scale_loads=[
                (s1sb[:], s1v), (s2sb[:], s2v), (sh1sb[:], sh1v),
                (sh2sb[:], sh2v), (s3sb[:], s3v),
            ], gp_loads=[(b["onesb"][:], ones_in)])
            tgt = s["LD1"].v
            p.emit("vector", lambda e, t=tgt: e.wait_ge(s["LD1"].h, t))
            p.emit("vector", lambda e: e.tensor_scalar(
                s3h[:], s3sb[:], 0.5, None, ALU.mult).then_inc(s["ONE"].h, 1))
            s["ONE"].bump()
            p.emit("scalar", lambda e, t=s["ONE"].v: e.wait_ge(s["ONE"].h, t))

            G = [0]
            slab_specs = []

            def mm_group(i, src, cols, nkc, rhs_fn, epi, pre_waits=()):
                g = G[0]
                G[0] += 1
                bank = g % 4
                if g >= NSLAB:
                    tgt = g - NSLAB + 1
                    p.emit("gpsimd", lambda e, t=tgt: e.wait_ge(s["GD"].h, t))
                p.emit("gpsimd", lambda e, sr=src, bu=g % NSLAB, cl=cols: e.dma_start(
                    out=wsl[:, bu, 0:cl], in_=sr).then_inc(s["WL"].h, 16))
                s["WL"].bump(16)
                p.emit("tensor", lambda e, t=16 * (g + 1): e.wait_ge(s["WL"].h, t))
                if bank_uses[bank] > 0:
                    p.emit("tensor", lambda e, bk=bank, t=bank_uses[bank]: e.wait_ge(
                        EP[bk].h, t))
                for sem_name, tgt in pre_waits:
                    p.emit("tensor", lambda e, sn=sem_name, t=tgt: e.wait_ge(s[sn].h, t))
                bu = g % NSLAB
                for kc in range(nkc):
                    last = kc == nkc - 1
                    p.emit("tensor", lambda e, kc=kc, bu=bu, bank=bank, last=last,
                           rf=rhs_fn: (
                        e.matmul(psm[bank][:], wsl[:, bu, kc * 128:(kc + 1) * 128],
                                 rf(kc), start=(kc == 0), stop=last
                                 ).then_inc(s["GD"].h, 1) if last else
                        e.matmul(psm[bank][:], wsl[:, bu, kc * 128:(kc + 1) * 128],
                                 rf(kc), start=(kc == 0), stop=last)))
                s["GD"].bump()
                epi(g, bank)
                bank_uses[bank] += 1

            for i in range(nt):
                par = i % 2
                emit_ln(p, s, b, i)
                xn_t = s["XN"].v

                for ot in range(32):
                    def epi_w1(g, bank, ot=ot, i=i):
                        tgt = s["GD"].v
                        p.emit("scalar", lambda e, t=tgt: e.wait_ge(s["GD"].h, t))
                        if i * 32 + ot >= 2:
                            tgt2 = i * 32 + ot - 1
                            p.emit("scalar", lambda e, t=tgt2: e.wait_ge(s["H0"].h, t))
                        p.emit("scalar", lambda e, ot=ot, bank=bank: e.activation(
                            g_s[:, ot % 2, :], psm[bank][:], AF.Silu,
                            scale=s1sb[:, ot:ot + 1]).then_inc(EP[bank].h, 1))
                        EP[bank].bump()
                    mm_group(i, w1s[ot], 1024, 8,
                             lambda kc, par=par: b["xnb"][:, par, kc, :], epi_w1,
                             pre_waits=[("XN", xn_t)])
                    def epi_w2(g, bank, ot=ot, i=i):
                        tgt = s["GD"].v
                        p.emit("vector", lambda e, t=tgt: e.wait_ge(s["GD"].h, t))
                        if i * 32 + ot >= 2:
                            tgt2 = i * 32 + ot - 1
                            p.emit("vector", lambda e, t=tgt2: e.wait_ge(s["H0"].h, t))
                        p.emit("vector", lambda e, ot=ot, bank=bank: e.tensor_scalar(
                            u_s[:, ot % 2, :], psm[bank][:], s2sb[:, ot:ot + 1],
                            None, ALU.mult).then_inc(EP[bank].h, 1))
                        EP[bank].bump()
                        tgt3 = EP[(bank - 1) % 4].v
                        p.emit("vector", lambda e, bk=(bank - 1) % 4, t=tgt3: e.wait_ge(
                            EP[bk].h, t))
                        p.emit("vector", lambda e, ot=ot: e.tensor_tensor(
                            h0[:, ot, :], g_s[:, ot % 2, :], u_s[:, ot % 2, :],
                            ALU.mult).then_inc(s["H0"].h, 1))
                        s["H0"].bump()
                    mm_group(i, w2s[ot], 1024, 8,
                             lambda kc, par=par: b["xnb"][:, par, kc, :], epi_w2,
                             pre_waits=[("XN", xn_t)])

                h0_t = s["H0"].v
                for o2 in range(32):
                    def epi_h1(g, bank, o2=o2):
                        tgt = s["GD"].v
                        p.emit("scalar", lambda e, t=tgt: e.wait_ge(s["GD"].h, t))
                        p.emit("scalar", lambda e, o2=o2, bank=bank: e.activation(
                            hg[:, o2, :], psm[bank][:], AF.Gelu,
                            scale=sh1sb[:, o2:o2 + 1]).then_inc(EP[bank].h, 1))
                        EP[bank].bump()
                    mm_group(i, h1s[o2], 4096, 32,
                             lambda kc: h0[:, kc, :], epi_h1,
                             pre_waits=[("H0", h0_t)])
                hg_done = [ep.v for ep in EP]
                for o2 in range(32):
                    def epi_h2(g, bank, o2=o2):
                        tgt = s["GD"].v
                        p.emit("scalar", lambda e, t=tgt: e.wait_ge(s["GD"].h, t))
                        p.emit("scalar", lambda e, o2=o2, bank=bank: e.activation(
                            h0[:, o2, :], psm[bank][:], AF.Gelu,
                            scale=sh2sb[:, o2:o2 + 1]).then_inc(EP[bank].h, 1))
                        EP[bank].bump()
                    mm_group(i, h2s[o2], 4096, 32,
                             lambda kc: hg[:, kc, :], epi_h2,
                             pre_waits=[(f"EP{k}", hg_done[k]) for k in range(4)]
                             if o2 == 0 else [])
                h2_done = [ep.v for ep in EP]
                for o3 in range(8):
                    def epi_w3(g, bank, o3=o3, i=i, par=par):
                        tgt = s["GD"].v
                        p.emit("vector", lambda e, t=tgt: e.wait_ge(s["GD"].h, t))
                        p.emit("vector", lambda e, o3=o3, bank=bank, par=par:
                               e.scalar_tensor_tensor(
                                   b["xbuf"][:, par, o3, :], psm[bank][:],
                                   s3h[:, o3:o3 + 1], b["xbuf"][:, par, o3, :],
                                   ALU.mult, ALU.add).then_inc(EP[bank].h, 1))
                        EP[bank].bump()
                    mm_group(i, w3s[o3], 4096, 32,
                             lambda kc: h0[:, kc, :], epi_w3,
                             pre_waits=[(f"EP{k}", h2_done[k]) for k in range(4)]
                             if o3 == 0 else [])
                emit_x_io(p, s, x_in, x_out, b["xbuf"], i,
                          epi_ready=[(EP[k], EP[k].v) for k in range(4)], nt=nt)
            emit_dma_drain(p, s)
            p.play(block)
    return nc


_TERN_CACHE = {}


def ternary_inputs(x_core, wd):
    out = {"x": x_core}
    out["w1s"] = wd["w1s"]; out["w2s"] = wd["w2s"]
    out["h1s"] = wd["h1s"]; out["h2s"] = wd["h2s"]; out["w3s"] = wd["w3s"]
    out["s1v"] = wd["s1v"]; out["s2v"] = wd["s2v"]
    out["sh1v"] = wd["sh1v"]; out["sh2v"] = wd["sh2v"]; out["s3v"] = wd["s3v"]
    out["ones_in"] = np.ones((128, 128), dtype=np.float32)
    return out


def prep_ternary_weights(w1_w, w1_s, w2_w, w2_s, h1_w, h1_s, h2_w, h2_s, w3_w, w3_s):
    return {
        "w1s": slab_major(np.ascontiguousarray(w1_w.T)),
        "w2s": slab_major(np.ascontiguousarray(w2_w.T)),
        "h1s": slab_major(np.ascontiguousarray(h1_w.T)),
        "h2s": slab_major(np.ascontiguousarray(h2_w.T)),
        "w3s": slab_major(np.ascontiguousarray(w3_w.T)),
        "s1v": col_major_vec(w1_s), "s2v": col_major_vec(w2_s),
        "sh1v": col_major_vec(h1_s), "sh2v": col_major_vec(h2_s),
        "s3v": col_major_vec(w3_s),
    }


def build_l1attn(nt=NT, phases=5):
    nc = bass.Bass("TRN2", target_bir_lowering=False, detect_race_conditions=False)
    x_in = nc.dram_tensor("x", [D, TCORE], F32, kind="ExternalInput").ap()
    x_out = nc.dram_tensor("xo", [D, TCORE], F32, kind="ExternalOutput").ap()
    w1s_in = nc.dram_tensor("w1s", [8, 128, 1024], F32, kind="ExternalInput").ap()
    w2s_in = nc.dram_tensor("w2s", [128, 128], F32, kind="ExternalInput").ap()
    w3s_in = nc.dram_tensor("w3s", [128, 128], F32, kind="ExternalInput").ap()
    b1_in = nc.dram_tensor("b1v", [128, 8], F32, kind="ExternalInput").ap()
    b2_in = nc.dram_tensor("b2v", [16, 1], F32, kind="ExternalInput").ap()
    b3_in = nc.dram_tensor("b3v", [16, 1], F32, kind="ExternalInput").ap()
    ones_in = nc.dram_tensor("ones_in", [128, 128], F32, kind="ExternalInput").ap()

    with contextlib.ExitStack() as stack:
        b = ln_bufs(stack, nc)
        sb = nc.sbuf_tensor
        w1sb = stack.enter_context(sb("w1sb", [128, 8, 1024], BF16))
        w2sb = stack.enter_context(sb("w2sb", [128, 128], BF16))
        w3sb = stack.enter_context(sb("w3sb", [128, 128], BF16))
        b1sb = stack.enter_context(sb("b1sb", [128, 8], F32))
        b2sb = stack.enter_context(sb("b2sb", [16, 1], F32))
        b3sb = stack.enter_context(sb("b3sb", [16, 1], F32))
        spA = stack.enter_context(sb("spA", [128, TT], F32))
        spB = stack.enter_context(sb("spB", [128, TT], F32))
        vm = stack.enter_context(sb("vm", [128, TT], F32))
        zb = stack.enter_context(sb("zb", [128, TT], F32))
        delta = stack.enter_context(sb("delta", [128, DC, TT], BF16))
        bms = stack.enter_context(sb("bms", [32, TT], BF16))
        cms = stack.enter_context(sb("cms", [32, TT], BF16))
        bc32 = stack.enter_context(sb("bc32", [32, TT], BF16))
        srow = stack.enter_context(sb("srow", [32, TT], BF16))
        t1b = stack.enter_context(sb("t1b", [128, TT], F32))
        t2b = stack.enter_context(sb("t2b", [128, TT], F32))
        psm = [stack.enter_context(nc.psum_tensor(f"psm{k}", [128, TT], F32))
               for k in range(4)]
        s = mk_sems(stack, nc, LN_SEMS + ["GD", "DL", "SR", "ONE",
                                          "EP0", "EP1", "EP2", "EP3"])
        EP = [s["EP0"], s["EP1"], s["EP2"], s["EP3"]]
        bank_uses = [0, 0, 0, 0]

        with nc.Block() as block:
            p = Prog(nc)
            emit_onetime(p, s, b, scale_loads=[
                (b1sb[:], b1_in), (b2sb[:], b2_in), (b3sb[:], b3_in),
            ], gp_loads=[
                (b["onesb"][:], ones_in), (w2sb[:], w2s_in), (w3sb[:], w3s_in),
            ] + [(w1sb[:, j, :], w1s_in[j]) for j in range(8)])
            p.emit("vector", lambda e: e.memset(bms[:], 0.0))
            p.emit("vector", lambda e: e.memset(cms[:], 0.0))
            p.emit("vector", lambda e: e.memset(srow[:], 0.0))

            G = [0]

            def bank_wait(bank):
                if bank_uses[bank] > 0:
                    p.emit("tensor", lambda e, bk=bank, t=EP[bank].v: e.wait_ge(
                        EP[bk].h, t))

            for i in range(nt):
                par = i % 2
                emit_ln(p, s, b, i)
                xn_t = s["XN"].v
                p.emit("tensor", lambda e, t=xn_t: e.wait_ge(s["XN"].h, t))
                if i == 0:
                    p.emit("tensor", lambda e, t=s["LD2"].v: e.wait_ge(s["LD2"].h, t))
                    p.emit("scalar", lambda e, t=s["LD1"].v: e.wait_ge(s["LD1"].h, t))
                    p.emit("vector", lambda e, t=s["LD1"].v: e.wait_ge(s["LD1"].h, t))

                for ot in range(DC if phases >= 1 else 0):
                    g = G[0]; G[0] += 1
                    bank = g % 4
                    bank_wait(bank)
                    for kc in range(DC):
                        p.emit("tensor", lambda e, kc=kc, ot=ot, bank=bank, par=par:
                               e.matmul(psm[bank][:], w1sb[:, ot, kc * 128:(kc + 1) * 128],
                                        b["xnb"][:, par, kc, :], start=(kc == 0),
                                        stop=(kc == DC - 1)).then_inc(s["GD"].h, 1)
                               if kc == DC - 1 else
                               e.matmul(psm[bank][:], w1sb[:, ot, kc * 128:(kc + 1) * 128],
                                        b["xnb"][:, par, kc, :], start=(kc == 0),
                                        stop=(kc == DC - 1)))
                    s["GD"].bump()
                    tgt = s["GD"].v
                    p.emit("scalar", lambda e, t=tgt: e.wait_ge(s["GD"].h, t))
                    if i * DC + ot >= 1:
                        p.emit("scalar", lambda e, t=s["DL"].v: e.wait_ge(s["DL"].h, t))
                    p.emit("scalar", lambda e, ot=ot, bank=bank: e.activation(
                        zb[:], psm[bank][:], AF.Identity,
                        bias=b1sb[:, ot:ot + 1]).then_inc(EP[bank].h, 1))
                    EP[bank].bump()
                    zb_t = EP[bank].v
                    p.emit("scalar", lambda e: e.activation(spA[:], zb[:], AF.Abs))
                    p.emit("scalar", lambda e: e.activation(spB[:], spA[:], AF.Exp, scale=-1.0))
                    p.emit("scalar", lambda e: e.activation(
                        spA[:], spB[:], AF.Ln, bias=1.0).then_inc(s["SR"].h, 1))
                    s["SR"].bump()
                    p.emit("vector", lambda e, bk=bank, t=zb_t: e.wait_ge(EP[bk].h, t))
                    p.emit("vector", lambda e: e.tensor_scalar(
                        vm[:], zb[:], 0.0, None, ALU.max))
                    p.emit("vector", lambda e, t=s["SR"].v: e.wait_ge(s["SR"].h, t))
                    p.emit("vector", lambda e, ot=ot: e.tensor_tensor(
                        delta[:, ot, :], vm[:], spA[:], ALU.add).then_inc(s["DL"].h, 1))
                    s["DL"].bump()
                    bank_uses[bank] += 1

                bc_in = []
                for wsb, bias_sb, dst in (((w2sb, b2sb, bms), (w3sb, b3sb, cms))
                                          if phases >= 2 else ()):
                    g = G[0]; G[0] += 1
                    bank = g % 4
                    bank_wait(bank)
                    bc_in.append(bank)
                    for kc in range(DC):
                        p.emit("tensor", lambda e, kc=kc, bank=bank, par=par, wsb=wsb:
                               e.matmul(psm[bank][0:16, :], wsb[:, kc * 16:(kc + 1) * 16],
                                        b["xnb"][:, par, kc, :], start=(kc == 0),
                                        stop=(kc == DC - 1)).then_inc(s["GD"].h, 1)
                               if kc == DC - 1 else
                               e.matmul(psm[bank][0:16, :], wsb[:, kc * 16:(kc + 1) * 16],
                                        b["xnb"][:, par, kc, :], start=(kc == 0),
                                        stop=(kc == DC - 1)))
                    s["GD"].bump()
                    tgt = s["GD"].v
                    p.emit("scalar", lambda e, t=tgt: e.wait_ge(s["GD"].h, t))
                    p.emit("scalar", lambda e, bank=bank, bias_sb=bias_sb, dst=dst:
                           e.activation(dst[0:16, :], psm[bank][0:16, :], AF.Identity,
                                        bias=bias_sb[:]).then_inc(EP[bank].h, 1))
                    EP[bank].bump()
                    bank_uses[bank] += 1
                if phases < 3:
                    emit_x_io(p, s, x_in, x_out, b["xbuf"], i, epi_ready=[], nt=nt)
                    continue
                for bk in bc_in:
                    p.emit("vector", lambda e, bk=bk, t=EP[bk].v: e.wait_ge(EP[bk].h, t))
                p.emit("vector", lambda e: e.tensor_tensor(
                    bc32[:], bms[:], cms[:], ALU.mult).then_inc(s["SR"].h, 1))
                s["SR"].bump()
                g = G[0]; G[0] += 1
                bank_s = g % 4
                bank_wait(bank_s)
                p.emit("tensor", lambda e, t=s["SR"].v: e.wait_ge(s["SR"].h, t))
                p.emit("tensor", lambda e, bank=bank_s: e.matmul(
                    psm[bank][0:1, :], b["onesb"][0:32, 0:1], bc32[:],
                    start=True, stop=True).then_inc(s["GD"].h, 1))
                s["GD"].bump()
                p.emit("vector", lambda e, t=s["GD"].v: e.wait_ge(s["GD"].h, t))
                p.emit("vector", lambda e, bank=bank_s: e.tensor_copy(
                    srow[0:1, :], psm[bank][0:1, :]).then_inc(EP[bank_s].h, 1))
                EP[bank_s].bump()
                bank_uses[bank_s] += 1
                if phases < 4:
                    emit_x_io(p, s, x_in, x_out, b["xbuf"], i, epi_ready=[], nt=nt)
                    continue
                g = G[0]; G[0] += 1
                bank_b = g % 4
                bank_wait(bank_b)
                p.emit("tensor", lambda e, t=EP[bank_s].v, bk=bank_s: e.wait_ge(EP[bk].h, t))
                p.emit("tensor", lambda e, bank=bank_b: e.matmul(
                    psm[bank][:], b["onesb"][0:32, :], srow[:],
                    start=True, stop=True).then_inc(s["GD"].h, 1))
                s["GD"].bump()
                if phases < 5:
                    p.emit("vector", lambda e, t=s["GD"].v: e.wait_ge(s["GD"].h, t))
                    p.emit("vector", lambda e, bank=bank_b: e.tensor_copy(
                        t2b[:], psm[bank][:]).then_inc(EP[bank_b].h, 1))
                    EP[bank_b].bump(); bank_uses[bank_b] += 1
                    emit_x_io(p, s, x_in, x_out, b["xbuf"], i, epi_ready=[], nt=nt)
                    continue
                p.emit("vector", lambda e, t=s["GD"].v: e.wait_ge(s["GD"].h, t))
                dl_t = s["DL"].v
                p.emit("vector", lambda e, t=dl_t: e.wait_ge(s["DL"].h, t))
                for dc in range(DC):
                    p.emit("vector", lambda e, dc=dc, par=par: e.tensor_tensor(
                        t1b[:], b["xnb"][:, par, dc, :], delta[:, dc, :], ALU.mult))
                    p.emit("vector", lambda e, dc=dc, bank=bank_b: e.tensor_tensor(
                        t2b[:], t1b[:], psm[bank][:], ALU.mult))
                    last = dc == DC - 1
                    p.emit("vector", lambda e, dc=dc, par=par, last=last: e.tensor_tensor(
                        b["xbuf"][:, par, dc, :], b["xbuf"][:, par, dc, :], t2b[:],
                        ALU.add).then_inc(EP[bank_b].h, 1) if last else
                        e.tensor_tensor(
                        b["xbuf"][:, par, dc, :], b["xbuf"][:, par, dc, :], t2b[:],
                        ALU.add))
                EP[bank_b].bump()
                bank_uses[bank_b] += 1
                emit_x_io(p, s, x_in, x_out, b["xbuf"], i,
                          epi_ready=[(EP[bank_b], EP[bank_b].v)], nt=nt)
            emit_dma_drain(p, s)
            p.play(block)
    return nc


def bcast16(t, ncols, free_extent):
    return bass.AP(t, 0, [[free_extent, 128], [1, ncols // 16], [0, 16]])


def build_l0ffn(nt=NT):
    nc = bass.Bass("TRN2", target_bir_lowering=False, detect_race_conditions=False)
    HI = 2816
    OT_H = HI // 128
    x_in = nc.dram_tensor("x", [D, TCORE], F32, kind="ExternalInput").ap()
    x_out = nc.dram_tensor("xo", [D, TCORE], F32, kind="ExternalOutput").ap()
    wg_in = nc.dram_tensor("wg", [HI, D], F32, kind="ExternalInput").ap()
    wu_in = nc.dram_tensor("wu", [HI, D], F32, kind="ExternalInput").ap()
    wd_in = nc.dram_tensor("wd", [D, HI], F32, kind="ExternalInput").ap()
    ones_in = nc.dram_tensor("ones_in", [128, 128], F32, kind="ExternalInput").ap()
    ident_in = nc.dram_tensor("ident_in", [128, 128], F32, kind="ExternalInput").ap()
    wqg = nc.dram_tensor("wqg", [OT_H, 128, D], BF16)
    wqu = nc.dram_tensor("wqu", [OT_H, 128, D], BF16)
    wqd = nc.dram_tensor("wqd", [8, 128, HI], BF16)

    NSLAB = 2
    with contextlib.ExitStack() as stack:
        b = ln_bufs(stack, nc, xn_dtype=F32, xn_bufs=1)
        sb = nc.sbuf_tensor
        h0 = stack.enter_context(sb("h0", [128, OT_H, TT], F32))
        hq = stack.enter_context(sb("hq", [128, OT_H, TT], BF16))
        xq = stack.enter_context(sb("xq", [128, DC, TT], BF16))
        wsl = stack.enter_context(sb("wsl", [128, NSLAB, HI], BF16))
        uq = stack.enter_context(sb("uq", [128, HI], F32))
        aq = stack.enter_context(sb("aq", [128, HI], F32))
        sg = stack.enter_context(sb("sg", [128, HI], BF16))
        t0q = stack.enter_context(sb("t0q", [128, HI], BF16))
        t1q = stack.enter_context(sb("t1q", [128, HI], BF16))
        t2q = stack.enter_context(sb("t2q", [128, HI], BF16))
        dqb = stack.enter_context(sb("dqb", [128, HI], BF16))
        bmq = stack.enter_context(sb("bmq", [128, HI // 16], F32))
        scq = stack.enter_context(sb("scq", [128, HI // 16], F32))
        rsq = stack.enter_context(sb("rsq", [128, HI // 16], F32))
        g_s = stack.enter_context(sb("g_s", [128, TT], F32))
        u_s = stack.enter_context(sb("u_s", [128, TT], F32))
        identb = stack.enter_context(sb("identb", [128, 128], BF16))
        identf = stack.enter_context(sb("identf", [128, 128], F32))
        psm = [stack.enter_context(nc.psum_tensor(f"psm{k}", [128, TT], F32))
               for k in range(2)]
        psT = [stack.enter_context(nc.psum_tensor(f"psT{k}", [128, 512], BF16))
               for k in range(2)]
        s = mk_sems(stack, nc, LN_SEMS + ["WL", "GD", "WQD", "WX", "QV", "QS", "QD",
                                          "H0", "EP0", "EP1", "TE0", "TE1"])
        EP = [s["EP0"], s["EP1"]]
        TE = [s["TE0"], s["TE1"]]
        bank_uses = [0, 0]
        tp_uses = [0, 0]
        G = [0]
        TPG = [0]
        SL = [0]
        slab_gd = []

        with nc.Block() as block:
            p = Prog(nc)
            emit_onetime(p, s, b, scale_loads=[], gp_loads=[
                (b["onesb"][:], ones_in), (identb[:], ident_in),
            ])
            p.emit("sync", lambda e: e.dma_start(
                out=identf[:], in_=ident_in).then_inc(s["LD1"].h, 16))
            s["LD1"].bump(16)

            def quant(src_cols, qv_waits=()):
                K = src_cols
                nb = K // 16
                for eng, sem, tgtv in qv_waits:
                    p.emit(eng, lambda e, sm=sem, t=tgtv: e.wait_ge(sm.h, t))
                p.emit("vector", lambda e, nb=nb: e.reduce_max(
                    bmq[:, 0:nb], bass.AP(uq, 0, [[HI, 128], [16, nb], [1, 16]]),
                    axis=AX.X, apply_absolute_value=True))
                p.emit("vector", lambda e, nb=nb: e.tensor_scalar(
                    scq[:, 0:nb], bmq[:, 0:nb], 1.0 / 6.0, 1e-12, ALU.mult, ALU.max))
                p.emit("vector", lambda e, nb=nb: e.reciprocal(
                    rsq[:, 0:nb], scq[:, 0:nb]))
                p.emit("vector", lambda e, K=K, nb=nb: e.tensor_tensor(
                    bass.AP(uq, 0, [[HI, 128], [16, nb], [1, 16]]),
                    bass.AP(uq, 0, [[HI, 128], [16, nb], [1, 16]]),
                    bcast16(rsq, K, HI // 16), ALU.mult).then_inc(s["QV"].h, 1))
                s["QV"].bump()
                p.emit("scalar", lambda e, t=s["QV"].v: e.wait_ge(s["QV"].h, t))
                p.emit("scalar", lambda e, K=K: e.activation(
                    aq[:, 0:K], uq[:, 0:K], AF.Square))
                p.emit("scalar", lambda e, K=K: e.activation(
                    sg[:, 0:K], uq[:, 0:K], AF.Sign).then_inc(s["QS"].h, 1))
                s["QS"].bump()
                p.emit("vector", lambda e, t=s["QS"].v: e.wait_ge(s["QS"].h, t))
                p.emit("vector", lambda e, K=K: e.tensor_scalar(
                    t0q[:, 0:K], aq[:, 0:K], 0.0625, None, ALU.is_gt))
                p.emit("vector", lambda e, K=K: e.scalar_tensor_tensor(
                    t1q[:, 0:K], aq[:, 0:K], 0.5625, t0q[:, 0:K], ALU.is_gt, ALU.add))
                p.emit("vector", lambda e, K=K: e.scalar_tensor_tensor(
                    t0q[:, 0:K], aq[:, 0:K], 1.5625, t1q[:, 0:K], ALU.is_gt, ALU.add))
                p.emit("vector", lambda e, K=K: e.scalar_tensor_tensor(
                    t1q[:, 0:K], aq[:, 0:K], 3.0625, t0q[:, 0:K], ALU.is_gt, ALU.add))
                p.emit("vector", lambda e, K=K: e.tensor_scalar(
                    t2q[:, 0:K], aq[:, 0:K], 6.25, None, ALU.is_gt))
                p.emit("vector", lambda e, K=K: e.scalar_tensor_tensor(
                    t0q[:, 0:K], aq[:, 0:K], 12.25, t2q[:, 0:K], ALU.is_gt, ALU.add))
                p.emit("vector", lambda e, K=K: e.scalar_tensor_tensor(
                    t2q[:, 0:K], aq[:, 0:K], 25.0, t0q[:, 0:K], ALU.is_gt, ALU.add))
                p.emit("vector", lambda e, K=K: e.scalar_tensor_tensor(
                    t0q[:, 0:K], aq[:, 0:K], 25.0, t2q[:, 0:K], ALU.is_gt, ALU.add))
                p.emit("vector", lambda e, K=K: e.scalar_tensor_tensor(
                    t2q[:, 0:K], t1q[:, 0:K], 0.5, t0q[:, 0:K], ALU.mult, ALU.add))
                p.emit("vector", lambda e, K=K: e.tensor_tensor(
                    t0q[:, 0:K], t2q[:, 0:K], sg[:, 0:K], ALU.mult))
                p.emit("vector", lambda e, K=K, nb=nb: e.tensor_tensor(
                    bass.AP(dqb, 0, [[HI, 128], [16, nb], [1, 16]]),
                    bass.AP(t0q, 0, [[HI, 128], [16, nb], [1, 16]]),
                    bcast16(scq, K, HI // 16), ALU.mult).then_inc(s["QD"].h, 1))
                s["QD"].bump()

            def t_f32(src_ap, dst_ap):
                g = G[0]; G[0] += 1
                bank = g % 2
                if bank_uses[bank] > 0:
                    p.emit("tensor", lambda e, bk=bank, t=EP[bank].v: e.wait_ge(EP[bk].h, t))
                p.emit("tensor", lambda e, bank=bank, sa=src_ap: e.transpose(
                    psm[bank][:, 0:128], sa, identf[:]).then_inc(s["GD"].h, 1))
                s["GD"].bump()
                p.emit("vector", lambda e, t=s["GD"].v: e.wait_ge(s["GD"].h, t))
                p.emit("vector", lambda e, bank=bank, da=dst_ap: e.tensor_copy(
                    da, psm[bank][:, 0:128]).then_inc(EP[bank].h, 1))
                EP[bank].bump()
                bank_uses[bank] += 1

            def t_bf16(src_ap, dst_ap):
                g = TPG[0]; TPG[0] += 1
                bank = g % 2
                if tp_uses[bank] > 0:
                    p.emit("tensor", lambda e, bk=bank, t=TE[bank].v: e.wait_ge(TE[bk].h, t))
                p.emit("tensor", lambda e, bank=bank, sa=src_ap: e.transpose(
                    psT[bank][:, 0:128], sa, identb[:]).then_inc(s["GD"].h, 1))
                s["GD"].bump()
                p.emit("vector", lambda e, t=s["GD"].v: e.wait_ge(s["GD"].h, t))
                p.emit("vector", lambda e, bank=bank, da=dst_ap: e.tensor_copy(
                    da, psT[bank][:, 0:128]).then_inc(TE[bank].h, 1))
                TE[bank].bump()
                tp_uses[bank] += 1

            p.emit("tensor", lambda e, t=s["LD1"].v: e.wait_ge(s["LD1"].h, t))
            p.emit("tensor", lambda e, t=s["LD2"].v: e.wait_ge(s["LD2"].h, t))
            for w_src, w_dst, nrt, K in ((wg_in, wqg, OT_H, D), (wu_in, wqu, OT_H, D),
                                         (wd_in, wqd, 8, HI)):
                for rt in range(nrt):
                    tgt_prev = s["QD"].v
                    p.emit("sync", lambda e, t=tgt_prev: e.wait_ge(s["QD"].h, t))
                    p.emit("sync", lambda e, t=s["WQD"].v: e.wait_ge(s["WQD"].h, t))
                    p.emit("sync", lambda e, ws=w_src, rt=rt, K=K: e.dma_start(
                        out=uq[:, 0:K], in_=ws[rt * 128:(rt + 1) * 128, :]).then_inc(
                        s["WX"].h, 16))
                    s["WX"].bump(16)
                    quant(K, qv_waits=[("vector", s["WX"], s["WX"].v)])
                    qd_t = s["QD"].v
                    first = True
                    for kc in range(K // 128):
                        if first:
                            p.emit("tensor", lambda e, t=qd_t: e.wait_ge(s["QD"].h, t))
                            first = False
                        t_bf16(dqb[:, kc * 128:(kc + 1) * 128],
                               wsl[:, 0, kc * 128:(kc + 1) * 128])
                    te_t = [TE[k].v for k in range(2)]
                    for k in range(2):
                        p.emit("sync", lambda e, k=k, t=te_t[k]: e.wait_ge(TE[k].h, t))
                    p.emit("sync", lambda e, wd=w_dst, rt=rt, K=K: e.dma_start(
                        out=wd.ap()[rt], in_=wsl[:, 0, 0:K]).then_inc(s["WQD"].h, 16))
                    s["WQD"].bump(16)
            wq_done = s["WQD"].v

            for i in range(nt):
                par = i % 2
                emit_ln(p, s, b, i, xn_dtype=F32)
                xn_t = s["XN"].v

                for sub in range(4):
                    qd_prev = s["QD"].v
                    first = True
                    for dc in range(DC):
                        if first:
                            p.emit("tensor", lambda e, t=xn_t: e.wait_ge(s["XN"].h, t))
                            p.emit("tensor", lambda e, t=qd_prev: e.wait_ge(s["QD"].h, t))
                            first = False
                        t_f32(b["xnb"][:, 0, dc, sub * 128:(sub + 1) * 128],
                              uq[:, dc * 128:(dc + 1) * 128])
                    ep_t = [EP[k].v for k in range(2)]
                    quant(1024, qv_waits=[("vector", EP[0], ep_t[0]),
                                          ("vector", EP[1], ep_t[1])])
                    qd_t = s["QD"].v
                    first = True
                    for dc in range(DC):
                        if first:
                            p.emit("tensor", lambda e, t=qd_t: e.wait_ge(s["QD"].h, t))
                            first = False
                        t_bf16(dqb[:, dc * 128:(dc + 1) * 128],
                               xq[:, dc, sub * 128:(sub + 1) * 128])
                xq_t = [TE[k].v for k in range(2)]

                def slab_load(src):
                    j = SL[0]; SL[0] += 1
                    bu = j % NSLAB
                    if j == 0:
                        p.emit("gpsimd", lambda e, t=wq_done: e.wait_ge(s["WQD"].h, t))
                    if j >= NSLAB:
                        p.emit("gpsimd", lambda e, t=slab_gd[j - NSLAB]: e.wait_ge(
                            s["GD"].h, t))
                    p.emit("gpsimd", lambda e, sr=src, bu=bu: e.dma_start(
                        out=wsl[:, bu, 0:sr.shape[-1]], in_=sr).then_inc(s["WL"].h, 16))
                    s["WL"].bump(16)
                    return bu, 16 * (j + 1)

                for ot in range(OT_H):
                    for which, wq_d in (("g", wqg), ("u", wqu)):
                        bu, wl_t = slab_load(wq_d.ap()[ot])
                        g = G[0]; G[0] += 1
                        bank = g % 2
                        p.emit("tensor", lambda e, t=wl_t: e.wait_ge(s["WL"].h, t))
                        if bank_uses[bank] > 0:
                            p.emit("tensor", lambda e, bk=bank, t=EP[bank].v:
                                   e.wait_ge(EP[bk].h, t))
                        if ot == 0 and which == "g":
                            for k in range(2):
                                p.emit("tensor", lambda e, k=k, t=xq_t[k]:
                                       e.wait_ge(TE[k].h, t))
                        for kc in range(DC):
                            last = kc == DC - 1
                            p.emit("tensor", lambda e, kc=kc, bu=bu, bank=bank, last=last:
                                   e.matmul(psm[bank][:], wsl[:, bu, kc * 128:(kc + 1) * 128],
                                            xq[:, kc, :], start=(kc == 0), stop=last
                                            ).then_inc(s["GD"].h, 1) if last else
                                   e.matmul(psm[bank][:], wsl[:, bu, kc * 128:(kc + 1) * 128],
                                            xq[:, kc, :], start=(kc == 0), stop=last))
                        s["GD"].bump()
                        slab_gd.append(s["GD"].v)
                        if which == "g":
                            p.emit("scalar", lambda e, t=s["GD"].v: e.wait_ge(s["GD"].h, t))
                            if i * OT_H + ot >= 1:
                                p.emit("scalar", lambda e, t=s["H0"].v: e.wait_ge(s["H0"].h, t))
                            p.emit("scalar", lambda e, bank=bank: e.activation(
                                g_s[:], psm[bank][:], AF.Silu).then_inc(EP[bank].h, 1))
                            EP[bank].bump()
                            bank_uses[bank] += 1
                        else:
                            p.emit("vector", lambda e, t=s["GD"].v: e.wait_ge(s["GD"].h, t))
                            if i * OT_H + ot >= 1:
                                p.emit("vector", lambda e, t=s["H0"].v: e.wait_ge(s["H0"].h, t))
                            p.emit("vector", lambda e, bank=bank: e.tensor_copy(
                                u_s[:], psm[bank][:]).then_inc(EP[bank].h, 1))
                            EP[bank].bump()
                            bank_uses[bank] += 1
                            p.emit("vector", lambda e, bk=(bank + 1) % 2,
                                   t=EP[(bank + 1) % 2].v: e.wait_ge(EP[bk].h, t))
                            p.emit("vector", lambda e, ot=ot: e.tensor_tensor(
                                h0[:, ot, :], g_s[:], u_s[:], ALU.mult).then_inc(
                                s["H0"].h, 1))
                            s["H0"].bump()

                h0_t = s["H0"].v
                for sub in range(4):
                    qd_prev = s["QD"].v
                    first = True
                    for kc in range(OT_H):
                        if first:
                            p.emit("tensor", lambda e, t=h0_t: e.wait_ge(s["H0"].h, t))
                            p.emit("tensor", lambda e, t=qd_prev: e.wait_ge(s["QD"].h, t))
                            first = False
                        t_f32(h0[:, kc, sub * 128:(sub + 1) * 128],
                              uq[:, kc * 128:(kc + 1) * 128])
                    ep_t = [EP[k].v for k in range(2)]
                    quant(HI, qv_waits=[("vector", EP[0], ep_t[0]),
                                        ("vector", EP[1], ep_t[1])])
                    qd_t = s["QD"].v
                    first = True
                    for kc in range(OT_H):
                        if first:
                            p.emit("tensor", lambda e, t=qd_t: e.wait_ge(s["QD"].h, t))
                            first = False
                        t_bf16(dqb[:, kc * 128:(kc + 1) * 128],
                               hq[:, kc, sub * 128:(sub + 1) * 128])
                hq_t = [TE[k].v for k in range(2)]

                for o3 in range(DC):
                    bu, wl_t = slab_load(wqd.ap()[o3])
                    g = G[0]; G[0] += 1
                    bank = g % 2
                    p.emit("tensor", lambda e, t=wl_t: e.wait_ge(s["WL"].h, t))
                    if bank_uses[bank] > 0:
                        p.emit("tensor", lambda e, bk=bank, t=EP[bank].v:
                               e.wait_ge(EP[bk].h, t))
                    if o3 == 0:
                        for k in range(2):
                            p.emit("tensor", lambda e, k=k, t=hq_t[k]:
                                   e.wait_ge(TE[k].h, t))
                    for kc in range(OT_H):
                        last = kc == OT_H - 1
                        p.emit("tensor", lambda e, kc=kc, bu=bu, bank=bank, last=last:
                               e.matmul(psm[bank][:], wsl[:, bu, kc * 128:(kc + 1) * 128],
                                        hq[:, kc, :], start=(kc == 0), stop=last
                                        ).then_inc(s["GD"].h, 1) if last else
                               e.matmul(psm[bank][:], wsl[:, bu, kc * 128:(kc + 1) * 128],
                                        hq[:, kc, :], start=(kc == 0), stop=last))
                    s["GD"].bump()
                    slab_gd.append(s["GD"].v)
                    p.emit("vector", lambda e, t=s["GD"].v: e.wait_ge(s["GD"].h, t))
                    p.emit("vector", lambda e, o3=o3, bank=bank, par=par:
                           e.scalar_tensor_tensor(
                               b["xbuf"][:, par, o3, :], psm[bank][:], 0.5,
                               b["xbuf"][:, par, o3, :], ALU.mult, ALU.add).then_inc(
                               EP[bank].h, 1))
                    EP[bank].bump()
                    bank_uses[bank] += 1
                emit_x_io(p, s, x_in, x_out, b["xbuf"], i,
                          epi_ready=[(EP[k], EP[k].v) for k in range(2)], nt=nt)
            emit_dma_drain(p, s)
            for nm in ("WQD", "WX"):
                if s[nm].v > 0:
                    p.emit("sync", lambda e, h=s[nm].h, t=s[nm].v: e.wait_ge(h, t))
            p.play(block)
    return nc


def build_attn(causal, out_proj, nt=NT):
    nc = bass.Bass("TRN2", target_bir_lowering=False, detect_race_conditions=False)
    SQ = 2048
    NH = 16
    x_in = nc.dram_tensor("x", [D, TCORE], F32, kind="ExternalInput").ap()
    x_out = nc.dram_tensor("xo", [D, TCORE], F32, kind="ExternalOutput").ap()
    wqh_in = nc.dram_tensor("wqh", [16, 128, 512], F32, kind="ExternalInput").ap()
    wkh_in = nc.dram_tensor("wkh", [16, 128, 512], F32, kind="ExternalInput").ap()
    wvh_in = nc.dram_tensor("wvh", [16, 128, 512], F32, kind="ExternalInput").ap()
    if out_proj:
        wos_in = nc.dram_tensor("wos", [8, 128, 1024], F32, kind="ExternalInput").ap()
    sbig_in = nc.dram_tensor("sbig", [128, 896], F32, kind="ExternalInput").ap()
    ones_in = nc.dram_tensor("ones_in", [128, 128], F32, kind="ExternalInput").ap()
    ident_in = nc.dram_tensor("ident_in", [128, 128], F32, kind="ExternalInput").ap()
    ao_dram = nc.dram_tensor("ao_scr", [D, TCORE], BF16)

    with contextlib.ExitStack() as stack:
        sb = nc.sbuf_tensor
        ps0 = stack.enter_context(nc.psum_tensor("ps0", [128, TT], F32))
        ps1 = stack.enter_context(nc.psum_tensor("ps1", [128, TT], F32))
        ps2 = stack.enter_context(nc.psum_tensor("ps2", [128, TT], F32))
        ps3 = stack.enter_context(nc.psum_tensor("ps3", [128, TT], F32))
        ps4 = stack.enter_context(nc.psum_tensor("ps4", [1, TT], F32))
        ps5 = stack.enter_context(nc.psum_tensor("ps5", [1, TT], F32))
        ps6 = stack.enter_context(nc.psum_tensor("ps6", [128, TT], F32))
        ps7 = stack.enter_context(nc.psum_tensor("ps7", [128, TT], F32))
        b = {}
        b["xn_bufs"] = 1
        b["x_bufs"] = 1
        b["xbuf"] = stack.enter_context(sb("xbuf", [128, 1, DC, TT], F32))
        xnf = stack.enter_context(sb("xnf", [128, DC, TCORE], BF16))
        b["xc"] = stack.enter_context(sb("xc", [128, 2, TT], BF16))
        b["x2c"] = stack.enter_context(sb("x2c", [128, 2, TT], BF16))
        b["stat_rhs"] = stack.enter_context(sb("stat_rhs", [32, 2 * TT], BF16))
        b["onesb"] = stack.enter_context(sb("onesb", [128, 128], BF16))
        b["tmp"] = stack.enter_context(sb("tmp", [128, TT], F32))
        for nm in ("mrow", "vrow", "m2row", "lnrow", "rrow"):
            b[nm] = stack.enter_context(sb(nm, [1, TT], F32))
        b["neghalf"] = stack.enter_context(sb("neghalf", [128, 1], F32))
        b["psA"], b["psB"], b["psBCA"], b["psBCB"] = ps4, ps5, ps2, ps3
        poshalf = stack.enter_context(sb("poshalf", [128, 1], F32))
        q_aug = stack.enter_context(sb("q_aug", [128, TCORE], BF16))
        k_aug = stack.enter_context(sb("k_aug", [128, TCORE], BF16))
        vh = stack.enter_context(sb("vh", [128, 32, 64], BF16))
        Ebuf = stack.enter_context(sb("Ebuf", [128, 16, TT], BF16))
        q2b = stack.enter_context(sb("q2b", [64, TCORE], BF16))
        nqrow = stack.enter_context(sb("nqrow", [1, TCORE], BF16))
        crow = stack.enter_context(sb("crow", [1, TCORE], BF16))
        km64 = stack.enter_context(sb("km64", [128, 1], F32))
        kmrow = stack.enter_context(sb("kmrow", [1, 128], F32))
        km1 = stack.enter_context(sb("km1", [1, 1], F32))
        zrow32 = stack.enter_context(sb("zrow32", [32, TT], BF16))
        ztmp = stack.enter_context(sb("ztmp", [1, TT], F32))
        rzb = stack.enter_context(sb("rzb", [64, TT], F32))
        aot = stack.enter_context(sb("aot", [64, TT], BF16))
        sbigb = stack.enter_context(sb("sbigb", [128, 896], BF16))
        wqsl = stack.enter_context(sb("wqsl", [128, 2, 512], BF16))
        wksl = stack.enter_context(sb("wksl", [128, 2, 512], BF16))
        wvsl = stack.enter_context(sb("wvsl", [128, 2, 512], BF16))
        aob = stack.enter_context(sb("aob", [128, 1, DC, TT], BF16))
        if out_proj:
            wosl = stack.enter_context(sb("wosl", [128, 2, 1024], BF16))
        identf = stack.enter_context(sb("identf", [128, 128], F32))
        s = mk_sems(stack, nc, LN_SEMS + [
            "WL", "GD", "QEV", "KEV", "VEV", "EPS0", "EPS1", "EMK", "EPZ", "EPA",
            "EPR", "CQ", "AODMA", "AOL", "RES"])
        psS = [ps0, ps1]
        EPS = [s["EPS0"], s["EPS1"]]
        eps_uses = [0, 0]

        with nc.Block() as block:
            p = Prog(nc)
            gp_loads = [(b["onesb"][:], ones_in), (sbigb[:], sbig_in)]
            emit_onetime(p, s, b, scale_loads=[(identf[:], ident_in)],
                         gp_loads=gp_loads)
            p.emit("vector", lambda e: e.memset(poshalf[:], 0.5))
            p.emit("vector", lambda e: e.memset(km64[:], 0.0))
            p.emit("vector", lambda e: e.memset(q_aug[:], 0.0))
            p.emit("vector", lambda e: e.memset(k_aug[:], 0.0))
            p.emit("vector", lambda e: e.memset(zrow32[:], 0.0))
            p.emit("vector", lambda e: e.memset(
                k_aug[64:65, :], 1.0).then_inc(s["CQ"].h, 1))
            s["CQ"].bump()

            def mm(bank_ps, lhsT_ap, rhs_ap, start, stop, inc=None):
                if inc is None:
                    p.emit("tensor", lambda e, bp=bank_ps, la=lhsT_ap, ra=rhs_ap,
                           st=start, sp=stop: e.matmul(bp, la, ra, start=st, stop=sp))
                else:
                    p.emit("tensor", lambda e, bp=bank_ps, la=lhsT_ap, ra=rhs_ap,
                           st=start, sp=stop, ic=inc: e.matmul(
                               bp, la, ra, start=st, stop=sp).then_inc(ic.h, 1))

            src_x = x_in.rearrange("(dc p) t -> p dc t", p=128)
            dst_x = x_out.rearrange("(dc p) t -> p dc t", p=128)
            for i in range(nt):
                if i > 0:
                    p.emit("sync", lambda e, t=i: e.wait_ge(s["XN"].h, t))
                p.emit("sync", lambda e, i=i: e.dma_start(
                    out=b["xbuf"][:, 0, :, :],
                    in_=src_x[:, :, i * TT:(i + 1) * TT]).then_inc(s["X"].h, 16))
                s["X"].bump(16)
                b["xnb"] = _XnFullView(xnf, i)
                emit_ln(p, s, b, i)
            xn_all = s["XN"].v
            stv_all = s["STV"].v

            first = [True]
            scores_done = [0]
            qkv_done = []
            for h in range(NH):
                hp = h % 2
                if h >= 2:
                    p.emit("gpsimd", lambda e, t=qkv_done[h - 2]: e.wait_ge(s["GD"].h, t))
                for wsl_t, wsrc in ((wqsl, wqh_in), (wksl, wkh_in), (wvsl, wvh_in)):
                    p.emit("gpsimd", lambda e, wt=wsl_t, wsr=wsrc, h=h, hp=hp:
                           e.dma_start(out=wt[:, hp, :], in_=wsr[h]).then_inc(
                               s["WL"].h, 16))
                    s["WL"].bump(16)
                wl_q, wl_k, wl_v = s["WL"].v - 32, s["WL"].v - 16, s["WL"].v
                for tc in range(nt):
                    if first[0] and tc == 0:
                        p.emit("tensor", lambda e, t=xn_all: e.wait_ge(s["XN"].h, t))
                        p.emit("tensor", lambda e, t=s["LD2"].v: e.wait_ge(s["LD2"].h, t))
                        first[0] = False
                    if tc == 0:
                        p.emit("tensor", lambda e, t=wl_q: e.wait_ge(s["WL"].h, t))
                    if tc == 0:
                        p.emit("tensor", lambda e, t=s["EPA"].v: e.wait_ge(s["EPA"].h, t))
                    p.emit("tensor", lambda e, t=s["QEV"].v: e.wait_ge(s["QEV"].h, t))
                    for kc in range(DC):
                        mm(ps6[0:64, :], wqsl[:, hp, kc * 64:(kc + 1) * 64],
                           xnf[:, kc, tc * TT:(tc + 1) * TT],
                           start=(kc == 0), stop=(kc == DC - 1),
                           inc=s["GD"] if kc == DC - 1 else None)
                    s["GD"].bump()
                    p.emit("vector", lambda e, t=s["GD"].v: e.wait_ge(s["GD"].h, t))
                    if h > 0 and tc == 0:
                        p.emit("vector", lambda e, t=scores_done[0]: e.wait_ge(s["GD"].h, t))
                    p.emit("vector", lambda e, tc=tc: e.tensor_scalar(
                        q_aug[0:64, tc * TT:(tc + 1) * TT], ps6[0:64, :],
                        0.125, None, ALU.mult).then_inc(s["QEV"].h, 1))
                    s["QEV"].bump()
                for tc in range(nt):
                    if tc == 0:
                        p.emit("tensor", lambda e, t=wl_k: e.wait_ge(s["WL"].h, t))
                    p.emit("tensor", lambda e, t=s["KEV"].v: e.wait_ge(s["KEV"].h, t))
                    for kc in range(DC):
                        mm(ps7[0:64, :], wksl[:, hp, kc * 64:(kc + 1) * 64],
                           xnf[:, kc, tc * TT:(tc + 1) * TT],
                           start=(kc == 0), stop=(kc == DC - 1),
                           inc=s["GD"] if kc == DC - 1 else None)
                    s["GD"].bump()
                    p.emit("vector", lambda e, t=s["GD"].v: e.wait_ge(s["GD"].h, t))
                    if h > 0 and tc == 0:
                        p.emit("vector", lambda e, t=scores_done[0]: e.wait_ge(s["GD"].h, t))
                    p.emit("vector", lambda e, tc=tc: e.tensor_copy(
                        k_aug[0:64, tc * TT:(tc + 1) * TT],
                        ps7[0:64, :]).then_inc(s["KEV"].h, 1))
                    s["KEV"].bump()
                for tc in range(32):
                    bps, bsem = (ps6, s["QEV"]) if tc % 2 == 0 else (ps7, s["KEV"])
                    if tc == 0:
                        p.emit("tensor", lambda e, t=wl_v: e.wait_ge(s["WL"].h, t))
                    p.emit("tensor", lambda e, bs=bsem, t=bsem.v: e.wait_ge(bs.h, t))
                    for kc in range(DC):
                        mm(bps[:, 0:64], xnf[:, kc, tc * 128:(tc + 1) * 128],
                           wvsl[:, hp, kc * 64:(kc + 1) * 64],
                           start=(kc == 0), stop=(kc == DC - 1))
                    mm(bps[0:1, 64:320], b["onesb"][0:1, 0:1],
                       bass.AP(b["onesb"], 0, [[128, 1], [0, 256]]),
                       start=True, stop=True, inc=s["GD"])
                    s["GD"].bump()
                    p.emit("vector", lambda e, t=s["GD"].v: e.wait_ge(s["GD"].h, t))
                    if h > 0 and tc < 2:
                        p.emit("vector", lambda e, t=scores_done[0]: e.wait_ge(s["GD"].h, t))
                    p.emit("vector", lambda e, tc=tc, bps=bps, bsem=bsem: e.tensor_copy(
                        vh[:, tc, :], bps[:, 0:64]).then_inc(bsem.h, 1))
                    bsem.bump()
                vev_q = s["QEV"].v
                vev_k = s["KEV"].v
                qkv_done.append(s["GD"].v)
                p.emit("scalar", lambda e, t=s["QEV"].v: e.wait_ge(s["QEV"].h, t))
                if h > 0:
                    p.emit("scalar", lambda e, t=scores_done[0]: e.wait_ge(s["GD"].h, t))
                p.emit("scalar", lambda e: e.activation(
                    q2b[:], q_aug[0:64, :], AF.Square).then_inc(s["CQ"].h, 1))
                s["CQ"].bump()
                for tc in range(nt):
                    p.emit("tensor", lambda e, t=s["CQ"].v: e.wait_ge(s["CQ"].h, t))
                    if h == 0 and tc == 0:
                        p.emit("tensor", lambda e, t=stv_all: e.wait_ge(s["STV"].h, t))
                    p.emit("tensor", lambda e, t=s["EPZ"].v: e.wait_ge(s["EPZ"].h, t))
                    mm(ps4[:], b["onesb"][0:64, 0:1], q2b[:, tc * TT:(tc + 1) * TT],
                       start=True, stop=True, inc=s["GD"])
                    s["GD"].bump()
                    p.emit("vector", lambda e, t=s["GD"].v: e.wait_ge(s["GD"].h, t))
                    p.emit("vector", lambda e, tc=tc: e.tensor_copy(
                        nqrow[0:1, tc * TT:(tc + 1) * TT], ps4[:]).then_inc(s["EPZ"].h, 1))
                    s["EPZ"].bump()
                p.emit("vector", lambda e, t=s["KEV"].v: e.wait_ge(s["KEV"].h, t))
                p.emit("vector", lambda e: e.reduce_max(
                    km64[0:64, :], k_aug[0:64, :], axis=AX.X,
                    apply_absolute_value=True).then_inc(s["CQ"].h, 1))
                s["CQ"].bump()
                p.emit("tensor", lambda e, t=s["CQ"].v: e.wait_ge(s["CQ"].h, t))
                if h == 0:
                    p.emit("tensor", lambda e, t=xn_all: e.wait_ge(s["XN"].h, t))
                    p.emit("tensor", lambda e, t=s["LD1"].v: e.wait_ge(s["LD1"].h, t))
                else:
                    p.emit("tensor", lambda e, t=s["EPR"].v: e.wait_ge(s["EPR"].h, t))
                p.emit("tensor", lambda e: e.transpose(
                    ps3[0:1, 0:128], km64[:, 0:1], identf[:]).then_inc(s["GD"].h, 1))
                s["GD"].bump()
                p.emit("vector", lambda e, t=s["GD"].v: e.wait_ge(s["GD"].h, t))
                p.emit("vector", lambda e: e.tensor_copy(
                    kmrow[0:1, 0:64], ps3[0:1, 0:64]))
                p.emit("vector", lambda e: e.reduce_max(
                    km1[:], kmrow[0:1, 0:64], axis=AX.X).then_inc(s["EPR"].h, 1))
                s["EPR"].bump()
                p.emit("scalar", lambda e, t=s["EPZ"].v: e.wait_ge(s["EPZ"].h, t))
                p.emit("scalar", lambda e: e.activation(crow[:], nqrow[:], AF.Ln))
                p.emit("scalar", lambda e: e.activation(
                    crow[:], crow[:], AF.Exp, scale=poshalf[0:1, :]).then_inc(
                    s["CQ"].h, 1))
                s["CQ"].bump()
                p.emit("vector", lambda e, t=s["CQ"].v: e.wait_ge(s["CQ"].h, t))
                p.emit("vector", lambda e, t=s["EPR"].v: e.wait_ge(s["EPR"].h, t))
                p.emit("vector", lambda e: e.tensor_scalar(
                    q_aug[64:65, :], crow[:], km1[0:1, :], -1.0,
                    ALU.mult, ALU.mult).then_inc(s["CQ"].h, 1))
                s["CQ"].bump()
                cq_row_t = s["CQ"].v

                for bq in range(2):
                    base = bq * SQ
                    for qt in range(4):
                        q0 = base + qt * TT
                        nkj = (qt * 4 + 4) if causal else 16
                        for kj in range(nkj):
                            sbank = kj % 2
                            kcol = base + kj * 128
                            if eps_uses[sbank] > 0:
                                p.emit("tensor", lambda e, sk=sbank, t=EPS[sbank].v:
                                       e.wait_ge(EPS[sk].h, t))
                            if kj == 0 and qt == 0 and bq == 0:
                                p.emit("tensor", lambda e, t=cq_row_t:
                                       e.wait_ge(s["CQ"].h, t))
                            mm(psS[sbank][:], k_aug[:, kcol:kcol + 128],
                               q_aug[:, q0:q0 + TT], start=True, stop=True,
                               inc=s["GD"])
                            s["GD"].bump()
                            p.emit("scalar", lambda e, t=s["GD"].v: e.wait_ge(s["GD"].h, t))
                            diag = causal and kj >= qt * 4
                            p.emit("scalar", lambda e, kj=kj, sk=sbank: e.activation(
                                Ebuf[:, kj, :], psS[sk][:], AF.Exp).then_inc(
                                EPS[sk].h, 1))
                            EPS[sbank].bump()
                            eps_uses[sbank] += 1
                            if diag:
                                doff = 384 - (kj - qt * 4) * 128
                                p.emit("vector", lambda e, sk=sbank, t=EPS[sbank].v:
                                       e.wait_ge(EPS[sk].h, t))
                                p.emit("vector", lambda e, kj=kj, doff=doff:
                                       e.tensor_tensor(
                                           Ebuf[:, kj, :], Ebuf[:, kj, :],
                                           sbigb[:, doff:doff + TT],
                                           ALU.mult).then_inc(s["EMK"].h, 1))
                                s["EMK"].bump()
                                p.emit("tensor", lambda e, t=s["EMK"].v:
                                       e.wait_ge(s["EMK"].h, t))
                            else:
                                p.emit("tensor", lambda e, sk=sbank, t=EPS[sbank].v:
                                       e.wait_ge(EPS[sk].h, t))
                            if kj == 0:
                                p.emit("tensor", lambda e, t=s["EPZ"].v: e.wait_ge(
                                    s["EPZ"].h, t))
                                p.emit("tensor", lambda e, t=s["EPA"].v: e.wait_ge(
                                    s["EPA"].h, t))
                                p.emit("tensor", lambda e, t=vev_q: e.wait_ge(
                                    s["QEV"].h, t))
                            mm(ps4[:], b["onesb"][:, 0:1], Ebuf[:, kj, :],
                               start=(kj == 0), stop=(kj == nkj - 1),
                               inc=s["GD"] if kj == nkj - 1 else None)
                            vtc = bq * 16 + kj
                            mm(ps6[0:64, :], vh[:, vtc, :], Ebuf[:, kj, :],
                               start=(kj == 0), stop=(kj == nkj - 1),
                               inc=s["GD"] if kj == nkj - 1 else None)
                        s["GD"].bump(2)
                        p.emit("vector", lambda e, t=s["GD"].v: e.wait_ge(s["GD"].h, t))
                        p.emit("vector", lambda e: e.tensor_scalar(
                            ztmp[:], ps4[:], 1e-30, None, ALU.max).then_inc(
                            s["EPZ"].h, 1))
                        s["EPZ"].bump()
                        def _rz(e):
                            with nc.allow_low_precision(
                                    reason="1/Z as bf16 softmax matmul operand"):
                                e.reciprocal(zrow32[0:1, :], ztmp[:]).then_inc(
                                    s["EPR"].h, 1)
                        p.emit("vector", _rz)
                        s["EPR"].bump()
                        p.emit("tensor", lambda e, t=s["EPR"].v: e.wait_ge(s["EPR"].h, t))
                        mm(ps3[0:64, :], b["onesb"][0:32, 0:64], zrow32[:],
                           start=True, stop=True, inc=s["GD"])
                        s["GD"].bump()
                        p.emit("vector", lambda e, t=s["GD"].v: e.wait_ge(s["GD"].h, t))
                        p.emit("vector", lambda e: e.tensor_copy(
                            rzb[:], ps3[0:64, :]).then_inc(s["EPR"].h, 1))
                        s["EPR"].bump()
                        p.emit("vector", lambda e, t=s["AODMA"].v: e.wait_ge(
                            s["AODMA"].h, t))
                        p.emit("vector", lambda e: e.scalar_tensor_tensor(
                            aot[:], ps6[0:64, :], 1.0, rzb[:], ALU.mult,
                            ALU.mult).then_inc(s["EPA"].h, 1))
                        s["EPA"].bump()
                        p.emit("sync", lambda e, t=s["EPA"].v: e.wait_ge(s["EPA"].h, t))
                        p.emit("sync", lambda e, h=h, q0=q0: e.dma_start(
                            out=ao_dram.ap()[h * 64:(h + 1) * 64, q0:q0 + TT],
                            in_=aot[:]).then_inc(s["AODMA"].h, 16))
                        s["AODMA"].bump(16)
                scores_done[0] = s["GD"].v

            ao_src = ao_dram.ap().rearrange("(dc p) t -> p dc t", p=128)
            wos_done = []
            for i in range(nt):
                if i == 0:
                    p.emit("sync", lambda e, t=xn_all: e.wait_ge(s["XN"].h, t))
                p.emit("sync", lambda e, t=s["ST"].v: e.wait_ge(s["ST"].h, t))
                p.emit("sync", lambda e, i=i: e.dma_start(
                    out=b["xbuf"][:, 0, :, :],
                    in_=src_x[:, :, i * TT:(i + 1) * TT]).then_inc(s["X"].h, 16))
                s["X"].bump(16)
                x_t = s["X"].v
                p.emit("sync", lambda e, t=s["AODMA"].v: e.wait_ge(s["AODMA"].h, t))
                if out_proj:
                    p.emit("sync", lambda e, t=s["EPS0"].v: e.wait_ge(s["EPS0"].h, t))
                    p.emit("sync", lambda e, t=s["EPS1"].v: e.wait_ge(s["EPS1"].h, t))
                else:
                    p.emit("sync", lambda e, t=s["RES"].v: e.wait_ge(s["RES"].h, t))
                p.emit("sync", lambda e, i=i: e.dma_start(
                    out=aob[:, 0, :, :],
                    in_=ao_src[:, :, i * TT:(i + 1) * TT]).then_inc(s["AOL"].h, 16))
                s["AOL"].bump(16)
                if out_proj:
                    for o3 in range(DC):
                        op = o3 % 2
                        gidx = i * DC + o3
                        if gidx >= 2:
                            p.emit("gpsimd", lambda e, t=wos_done[gidx - 2]:
                                   e.wait_ge(s["GD"].h, t))
                        p.emit("gpsimd", lambda e, o3=o3, op=op: e.dma_start(
                            out=wosl[:, op, :], in_=wos_in[o3]).then_inc(s["WL"].h, 16))
                        s["WL"].bump(16)
                        wl_t = s["WL"].v
                        bank, bsem = (ps0, s["EPS0"]) if o3 % 2 == 0 else (ps1, s["EPS1"])
                        p.emit("tensor", lambda e, t=wl_t: e.wait_ge(s["WL"].h, t))
                        p.emit("tensor", lambda e, t=s["AOL"].v: e.wait_ge(s["AOL"].h, t))
                        p.emit("tensor", lambda e, bs=bsem, t=bsem.v: e.wait_ge(bs.h, t))
                        for kc in range(DC):
                            mm(bank[:], wosl[:, op, kc * 128:(kc + 1) * 128],
                               aob[:, 0, kc, :], start=(kc == 0), stop=(kc == DC - 1),
                               inc=s["GD"] if kc == DC - 1 else None)
                        s["GD"].bump()
                        wos_done.append(s["GD"].v)
                        p.emit("vector", lambda e, t=s["GD"].v: e.wait_ge(s["GD"].h, t))
                        p.emit("vector", lambda e, t=x_t: e.wait_ge(s["X"].h, t))
                        p.emit("vector", lambda e, o3=o3, bank=bank, bsem=bsem:
                               e.scalar_tensor_tensor(
                                   b["xbuf"][:, 0, o3, :], bank[:], 1.0,
                                   b["xbuf"][:, 0, o3, :], ALU.mult, ALU.add
                               ).then_inc(bsem.h, 1))
                        bsem.bump()
                    res_waits = [(s["EPS0"], s["EPS0"].v), (s["EPS1"], s["EPS1"].v)]
                else:
                    p.emit("vector", lambda e, t=s["AOL"].v: e.wait_ge(s["AOL"].h, t))
                    p.emit("vector", lambda e, t=x_t: e.wait_ge(s["X"].h, t))
                    for dc in range(DC):
                        last = dc == DC - 1
                        p.emit("vector", lambda e, dc=dc, last=last:
                               e.tensor_tensor(
                                   b["xbuf"][:, 0, dc, :], b["xbuf"][:, 0, dc, :],
                                   aob[:, 0, dc, :], ALU.add).then_inc(
                                   s["RES"].h, 1) if last else
                               e.tensor_tensor(
                                   b["xbuf"][:, 0, dc, :], b["xbuf"][:, 0, dc, :],
                                   aob[:, 0, dc, :], ALU.add))
                    s["RES"].bump()
                    res_waits = [(s["RES"], s["RES"].v)]
                for sem, tgt in res_waits:
                    p.emit("sync", lambda e, sem=sem, t=tgt: e.wait_ge(sem.h, t))
                p.emit("sync", lambda e, i=i: e.dma_start(
                    out=dst_x[:, :, i * TT:(i + 1) * TT],
                    in_=b["xbuf"][:, 0, :, :]).then_inc(s["ST"].h, 16))
                s["ST"].bump(16)
            emit_dma_drain(p, s)
            for nm in ("AODMA", "AOL"):
                if s[nm].v > 0:
                    p.emit("sync", lambda e, hh=s[nm].h, t=s[nm].v: e.wait_ge(hh, t))
            p.play(block)
    return nc


class _XnFullView:
    def __init__(self, xnf, i):
        self.xnf = xnf
        self.i = i

    def __getitem__(self, idx):
        _, _, c, tsl = idx
        if tsl == slice(None):
            return self.xnf[:, c, self.i * TT:(self.i + 1) * TT]
        raise KeyError(tsl)


def build_moe(nt=NT, dbg=False, dbg2=False):
    nc = bass.Bass("TRN2", target_bir_lowering=False, detect_race_conditions=False)
    E = 8
    CAP = 100
    x_in = nc.dram_tensor("x", [D, TCORE], F32, kind="ExternalInput").ap()
    x_out = nc.dram_tensor("xo", [D, TCORE], F32, kind="ExternalOutput").ap()
    wes_in = nc.dram_tensor("wes", [8, 128, 1024], F32, kind="ExternalInput").ap()
    wgr_in = nc.dram_tensor("wgr", [128, 8, 8], F32, kind="ExternalInput").ap()
    bev_in = nc.dram_tensor("bev", [128, 8], F32, kind="ExternalInput").ap()
    rowsel_in = nc.dram_tensor("rowsel", [1, 8], F32, kind="ExternalInput").ap()
    prevm_in = nc.dram_tensor("prevm", [32, 1], F32, kind="ExternalInput").ap()
    ones_in = nc.dram_tensor("ones_in", [128, 128], F32, kind="ExternalInput").ap()
    ident_in = nc.dram_tensor("ident_in", [128, 128], F32, kind="ExternalInput").ap()
    cc_in = nc.dram_tensor("cc_in", [8, 8], F32)
    cc_out = nc.dram_tensor("cc_out", [8, 8], F32)
    offs_dram = nc.dram_tensor("offs_scr", [8, 1], F32)
    if dbg2:
        d2_offs = nc.dram_tensor("d2_offs", [8, 1], F32, kind="ExternalOutput").ap()
        d2_keep = nc.dram_tensor("d2_keep", [32, TCORE], F32, kind="ExternalOutput").ap()
        d2_gate = nc.dram_tensor("d2_gate", [1, TCORE], F32, kind="ExternalOutput").ap()
        d2_mask = nc.dram_tensor("d2_mask", [32, TCORE], F32, kind="ExternalOutput").ap()
        d2_loc = nc.dram_tensor("d2_loc", [32, TCORE], F32, kind="ExternalOutput").ap()
    if dbg:
        dbg_lg = nc.dram_tensor("dbg_lg", [128, 256], F32, kind="ExternalOutput").ap()
        dbg_lg2 = nc.dram_tensor("dbg_lg2", [128, 256], F32, kind="ExternalOutput").ap()
        dbg_gm = nc.dram_tensor("dbg_gm", [128, 256], F32, kind="ExternalOutput").ap()
        dbg_z = nc.dram_tensor("dbg_z", [128, 32], F32, kind="ExternalOutput").ap()
        dbg_nm = nc.dram_tensor("dbg_nm", [128, 32], F32, kind="ExternalOutput").ap()
        dbg_gate = nc.dram_tensor("dbg_gate", [1, TCORE], F32, kind="ExternalOutput").ap()
        dbg_mask = nc.dram_tensor("dbg_mask", [32, TCORE], F32, kind="ExternalOutput").ap()
        dbg_keep = nc.dram_tensor("dbg_keep", [32, TCORE], F32, kind="ExternalOutput").ap()
        dbg_call = nc.dram_tensor("dbg_call", [32, 8], F32, kind="ExternalOutput").ap()
        dbg_wg = nc.dram_tensor("dbg_wg", [128, 8, 8], F32, kind="ExternalOutput").ap()
        dbg_xn = nc.dram_tensor("dbg_xn", [128, 8, TCORE], F32, kind="ExternalOutput").ap()

    with contextlib.ExitStack() as stack:
        sb = nc.sbuf_tensor
        ps = [stack.enter_context(nc.psum_tensor(f"ps{k}", [128, TT], F32))
              for k in range(8)]
        b = {}
        b["xn_bufs"] = 1
        b["xbuf"] = stack.enter_context(sb("xbuf", [128, 2, DC, TT], F32))
        xnf = stack.enter_context(sb("xnf", [128, DC, TCORE], BF16))
        b["xc"] = stack.enter_context(sb("xc", [128, 2, TT], BF16))
        b["x2c"] = stack.enter_context(sb("x2c", [128, 2, TT], BF16))
        b["stat_rhs"] = stack.enter_context(sb("stat_rhs", [32, 2 * TT], BF16))
        b["onesb"] = stack.enter_context(sb("onesb", [128, 128], BF16))
        b["tmp"] = stack.enter_context(sb("tmp", [128, TT], F32))
        for nm in ("mrow", "vrow", "m2row", "lnrow", "rrow"):
            b[nm] = stack.enter_context(sb(nm, [1, TT], F32))
        b["neghalf"] = stack.enter_context(sb("neghalf", [128, 1], F32))
        b["psA"] = ps[0][0:1, :]
        b["psB"] = ps[1][0:1, :]
        b["psBCA"] = ps[2]
        b["psBCB"] = ps[3]
        lsb = stack.enter_context(sb("lsb", [128, 8], F32))
        lmax = stack.enter_context(sb("lmax", [128, 1], F32))
        nmax = stack.enter_context(sb("nmax", [128, 1], F32))
        gmx = stack.enter_context(sb("gmx", [128, 8], F32))
        Zg = stack.enter_context(sb("Zg", [128, 1], F32))
        Zr = stack.enter_context(sb("Zr", [128, 1], F32))
        gm = stack.enter_context(sb("gm", [128, 64], F32))
        Mask32 = stack.enter_context(sb("Mask32", [32, TCORE], F32))
        gate_row = stack.enter_context(sb("gate_row", [1, TCORE], F32))
        onesM = stack.enter_context(sb("onesM", [8, 1], F32))
        init0 = stack.enter_context(sb("init0", [8, 1], F32))
        loc = stack.enter_context(sb("loc", [32, TCORE], F32))
        ccs = stack.enter_context(sb("ccs", [8, 8], F32))
        call = stack.enter_context(sb("call", [32, 8], F32))
        cntrow = stack.enter_context(sb("cntrow", [1, 8], F32))
        orow = stack.enter_context(sb("orow", [1, 8], F32))
        offs = stack.enter_context(sb("offs", [8, 1], F32))
        offs2 = stack.enter_context(sb("offs2", [8, 1], F32))
        rowsel_sb = stack.enter_context(sb("rowsel_sb", [1, 8], F32))
        prevm_sb = stack.enter_context(sb("prevm_sb", [32, 1], F32))
        grow32 = stack.enter_context(sb("grow32", [32, TT], BF16))
        esb = stack.enter_context(sb("esb", [128, TT], F32))
        t2b = stack.enter_context(sb("t2b", [128, TT], F32))
        wesb = stack.enter_context(sb("wesb", [128, 8, 1024], BF16))
        wgsb = stack.enter_context(sb("wgsb", [128, 8, 8], BF16))
        besb = stack.enter_context(sb("besb", [128, 8], F32))
        km1x = stack.enter_context(sb("km1x", [1, 1], F32))
        fbuf = stack.enter_context(sb("fbuf", [128, 4096], BF16))
        if dbg:
            lgall = stack.enter_context(sb("lgall", [128, 256], F32))
            lgall2 = stack.enter_context(sb("lgall2", [128, 256], F32))
            gmall = stack.enter_context(sb("gmall", [128, 256], F32))
            zall = stack.enter_context(sb("zall", [128, 32], F32))
            nmall = stack.enter_context(sb("nmall", [128, 32], F32))
        identf = stack.enter_context(sb("identf", [128, 128], F32))
        keepb = stack.enter_context(sb("keepb", [32, TCORE], BF16))
        s = mk_sems(stack, nc, LN_SEMS + [
            "GD", "GTV", "GTS", "GEV", "TEV", "CCD", "CCC", "PH2", "KP", "GPD",
            "EPK", "EPG", "EPE0", "EPE1", "RES"])
        psL, psT = ps[6], ps[7]
        psE = [ps[4], ps[5]]
        EPE = [s["EPE0"], s["EPE1"]]
        epe_uses = [0, 0]

        with nc.Block() as block:
            p = Prog(nc)
            emit_onetime(p, s, b, scale_loads=[
                (identf[:], ident_in), (besb[:], bev_in),
            ], gp_loads=[
                (b["onesb"][:], ones_in), (wgsb[:], wgr_in),
            ] + [(wesb[:, j, :], wes_in[j]) for j in range(8)])
            p.emit("sync", lambda e: e.dma_start(
                out=rowsel_sb[:], in_=rowsel_in).then_inc(s["LD1"].h, 16))
            s["LD1"].bump(16)
            p.emit("sync", lambda e: e.dma_start(
                out=prevm_sb[:], in_=prevm_in).then_inc(s["LD1"].h, 16))
            s["LD1"].bump(16)
            p.emit("vector", lambda e: e.memset(gm[:], 0.0))
            p.emit("vector", lambda e: e.memset(Mask32[:], 0.0))
            p.emit("vector", lambda e: e.memset(onesM[:], 1.0))
            p.emit("vector", lambda e: e.memset(init0[:], 0.0))
            p.emit("vector", lambda e: e.memset(loc[:], 0.0))
            p.emit("vector", lambda e: e.memset(call[:], 0.0))
            p.emit("vector", lambda e: e.memset(grow32[:], 0.0))
            p.emit("vector", lambda e: e.memset(keepb[:], 0.0))
            p.emit("vector", lambda e: e.memset(fbuf[:], 0.0))

            def mm(bank_ps, lhsT_ap, rhs_ap, start, stop, inc=None):
                if inc is None:
                    p.emit("tensor", lambda e, bp=bank_ps, la=lhsT_ap, ra=rhs_ap,
                           st=start, sp=stop: e.matmul(bp, la, ra, start=st, stop=sp))
                else:
                    p.emit("tensor", lambda e, bp=bank_ps, la=lhsT_ap, ra=rhs_ap,
                           st=start, sp=stop, ic=inc: e.matmul(
                               bp, la, ra, start=st, stop=sp).then_inc(ic.h, 1))

            src_x = x_in.rearrange("(dc p) t -> p dc t", p=128)
            dst_x = x_out.rearrange("(dc p) t -> p dc t", p=128)

            for i in range(nt):
                par = i % 2
                if i == 0:
                    for j in range(min(2, nt)):
                        p.emit("sync", lambda e, j=j: e.dma_start(
                            out=b["xbuf"][:, j, :, :],
                            in_=src_x[:, :, j * TT:(j + 1) * TT]).then_inc(s["X"].h, 16))
                        s["X"].bump(16)
                b["xnb"] = _XnFullView(xnf, i)
                emit_ln(p, s, b, i)
                if i + 2 < nt:
                    p.emit("sync", lambda e, t=i + 1: e.wait_ge(s["XN"].h, t))
                    p.emit("sync", lambda e, par=par, i=i: e.dma_start(
                        out=b["xbuf"][:, par, :, :],
                        in_=src_x[:, :, (i + 2) * TT:(i + 3) * TT]).then_inc(
                        s["X"].h, 16))
                    s["X"].bump(16)
                xn_t = s["XN"].v
                for c in range(4):
                    ch = i * 4 + c
                    col = ch * 128
                    p.emit("tensor", lambda e, t=xn_t: e.wait_ge(s["XN"].h, t))
                    if ch == 0:
                        p.emit("tensor", lambda e, t=s["LD2"].v: e.wait_ge(s["LD2"].h, t))
                    else:
                        p.emit("tensor", lambda e, t=s["GEV"].v: e.wait_ge(s["GEV"].h, t))
                    for dc in range(DC):
                        mm(psL[:, 0:8], xnf[:, dc, col:col + 128], wgsb[:, dc, :],
                           start=(dc == 0), stop=(dc == DC - 1))
                    mm(psL[0:1, 8:264], b["onesb"][0:1, 0:1], b["onesb"][0:1, 0:1].rearrange("a b -> a b") if False else bass.AP(b["onesb"], 0, [[128, 1], [0, 256]]),
                       start=True, stop=True, inc=s["GD"])
                    s["GD"].bump()
                    p.emit("vector", lambda e, t=s["GD"].v: e.wait_ge(s["GD"].h, t))
                    p.emit("vector", lambda e: e.tensor_copy(lsb[:], psL[:, 0:8]))
                    if dbg:
                        p.emit("vector", lambda e, ch=ch: e.tensor_copy(
                            lgall[:, ch * 8:(ch + 1) * 8], lsb[:]))
                    emit_fence(p, "vector", fbuf)
                    emit_fence(p, "vector", fbuf)
                    emit_fence(p, "vector", fbuf)
                    p.emit("vector", lambda e: e.reduce_max(
                        lmax[:], lsb[:], axis=AX.X).then_inc(s["GEV"].h, 1))
                    s["GEV"].bump()
                    emit_fence(p, "vector", fbuf, s["GTV"])
                    p.emit("scalar", lambda e, t=s["GTV"].v: e.wait_ge(s["GTV"].h, t))
                    p.emit("scalar", lambda e: e.activation(gmx[:], lsb[:], AF.Exp))
                    emit_fence(p, "scalar", fbuf, s["GTS"])
                    p.emit("vector", lambda e, t=s["GTS"].v: e.wait_ge(s["GTS"].h, t))
                    if ch > 0:
                        p.emit("vector", lambda e, t=s["TEV"].v: e.wait_ge(s["TEV"].h, t))
                        p.emit("vector", lambda e, t=s["GPD"].v: e.wait_ge(s["GPD"].h, t))
                    p.emit("vector", lambda e: e.reduce_sum(Zg[:], gmx[:], axis=AX.X))
                    p.emit("vector", lambda e: e.reduce_max(nmax[:], gmx[:], axis=AX.X))
                    if dbg:
                        p.emit("vector", lambda e, ch=ch: e.tensor_copy(
                            gmall[:, ch * 8:(ch + 1) * 8], gmx[:]))
                        p.emit("vector", lambda e, ch=ch: e.tensor_copy(
                            zall[:, ch:ch + 1], Zg[:]))
                        p.emit("vector", lambda e, ch=ch: e.tensor_copy(
                            nmall[:, ch:ch + 1], nmax[:]))
                    p.emit("vector", lambda e: e.tensor_tensor(
                        gm[:, 0:8], lsb[:],
                        bass.AP(lmax, 0, [[1, 128], [0, 8]]), ALU.is_ge))
                    emit_fence(p, "vector", fbuf)
                    emit_fence(p, "vector", fbuf)
                    emit_fence(p, "vector", fbuf)
                    p.emit("vector", lambda e: e.reciprocal(Zr[:], Zg[:]))
                    emit_fence(p, "vector", fbuf, s["GTV"])
                    p.emit("gpsimd", lambda e, t=s["GTV"].v: e.wait_ge(s["GTV"].h, t))
                    p.emit("gpsimd", lambda e: e.tensor_tensor(
                        gm[:, 32:33], nmax[:], Zr[:], ALU.mult))
                    emit_fence(p, "gpsimd", fbuf, s["GPD"])
                    p.emit("tensor", lambda e, t=s["GPD"].v: e.wait_ge(s["GPD"].h, t))
                    if dbg:
                        p.emit("vector", lambda e, ch=ch: e.tensor_copy(
                            lgall2[:, ch * 8:(ch + 1) * 8], psL[:, 0:8]))
                    p.emit("tensor", lambda e, t=s["GTV"].v: e.wait_ge(s["GTV"].h, t))
                    if ch == 0:
                        p.emit("tensor", lambda e, t=s["LD1"].v: e.wait_ge(s["LD1"].h, t))
                    else:
                        p.emit("tensor", lambda e, t=s["TEV"].v: e.wait_ge(s["TEV"].h, t))
                    p.emit("tensor", lambda e: e.transpose(
                        psT[0:64, 0:128], gm[:], identf[:]))
                    mm(psT[0:1, 136:392], b["onesb"][0:1, 0:1],
                       bass.AP(b["onesb"], 0, [[128, 1], [0, 256]]),
                       start=True, stop=True, inc=s["GD"])
                    s["GD"].bump()
                    p.emit("vector", lambda e, t=s["GD"].v: e.wait_ge(s["GD"].h, t))
                    p.emit("vector", lambda e, col=col: e.tensor_copy(
                        Mask32[0:8, col:col + 128], psT[0:8, 0:128]))
                    p.emit("vector", lambda e, col=col: e.tensor_copy(
                        gate_row[0:1, col:col + 128], psT[32:33, 0:128]))
                    p.emit("vector", lambda e: e.tensor_copy(
                        km1x[:], gm[0:1, 0:1]).then_inc(s["TEV"].h, 1))
                    s["TEV"].bump()

            p.emit("vector", lambda e, t=s["TEV"].v: e.wait_ge(s["TEV"].h, t))
            p.emit("vector", lambda e: e.tensor_tensor_scan(
                loc[0:8, :], bass.AP(onesM, 0, [[1, 8], [0, TCORE]]),
                Mask32[0:8, :], init0[:], ALU.mult, ALU.add))
            emit_fence(p, "vector", fbuf, s["KP"])
            p.emit("tensor", lambda e, t=s["KP"].v: e.wait_ge(s["KP"].h, t))
            p.emit("tensor", lambda e: e.transpose(
                psT[0:1, 0:8], loc[0:8, TCORE - 1:TCORE], identf[0:8, 0:8]))
            mm(psT[0:1, 136:392], b["onesb"][0:1, 0:1],
               bass.AP(b["onesb"], 0, [[128, 1], [0, 256]]),
               start=True, stop=True, inc=s["GD"])
            s["GD"].bump()
            p.emit("vector", lambda e, t=s["GD"].v: e.wait_ge(s["GD"].h, t))
            p.emit("vector", lambda e: e.tensor_copy(cntrow[:], psT[0:1, 0:8]))
            emit_fence(p, "vector", fbuf, s["KP"])
            p.emit("tensor", lambda e, t=s["KP"].v: e.wait_ge(s["KP"].h, t))
            p.emit("tensor", lambda e: e.matmul(
                psL[0:8, 0:8], rowsel_sb[:], cntrow[:], start=True, stop=True))
            mm(psL[0:1, 8:264], b["onesb"][0:1, 0:1],
               bass.AP(b["onesb"], 0, [[128, 1], [0, 256]]),
               start=True, stop=True, inc=s["GD"])
            s["GD"].bump()
            p.emit("vector", lambda e, t=s["GD"].v: e.wait_ge(s["GD"].h, t))
            p.emit("vector", lambda e: e.tensor_copy(ccs[:], psL[0:8, 0:8]))
            emit_fence(p, "vector", fbuf, s["KP"])
            p.emit("gpsimd", lambda e, t=s["KP"].v: e.wait_ge(s["KP"].h, t))
            p.emit("gpsimd", lambda e: e.dma_start(
                out=cc_in.ap(), in_=ccs[:]).then_inc(s["CCD"].h, 16))
            s["CCD"].bump(16)
            p.emit("gpsimd", lambda e, t=s["CCD"].v: e.wait_ge(s["CCD"].h, t))
            p.emit("gpsimd", lambda e: e.collective_compute(
                "AllReduce", ALU.add, replica_groups=[list(range(NCORES))],
                ins=[cc_in.ap().opt()], outs=[cc_out.ap().opt()]).then_inc(s["CCC"].h, 1))
            s["CCC"].bump(1)
            p.emit("gpsimd", lambda e, t=s["CCC"].v: e.wait_ge(s["CCC"].h, t))
            p.emit("gpsimd", lambda e: e.dma_start(
                out=call[0:8, :], in_=cc_out.ap()).then_inc(s["CCD"].h, 16))
            s["CCD"].bump(16)
            p.emit("tensor", lambda e, t=s["CCD"].v: e.wait_ge(s["CCD"].h, t))
            p.emit("tensor", lambda e: e.matmul(
                psL[0:1, 0:8], prevm_sb[:], call[:], start=True, stop=True))
            mm(psL[0:1, 8:264], b["onesb"][0:1, 0:1],
               bass.AP(b["onesb"], 0, [[128, 1], [0, 256]]),
               start=True, stop=True, inc=s["GD"])
            s["GD"].bump()
            p.emit("vector", lambda e, t=s["GD"].v: e.wait_ge(s["GD"].h, t))
            p.emit("vector", lambda e: e.tensor_copy(orow[:], psL[0:1, 0:8]))
            emit_fence(p, "vector", fbuf, s["KP"])
            p.emit("tensor", lambda e, t=s["KP"].v: e.wait_ge(s["KP"].h, t))
            p.emit("tensor", lambda e: e.transpose(
                psT[0:8, 0:1], orow[:], identf[0:1, 0:1]))
            mm(psT[0:1, 136:392], b["onesb"][0:1, 0:1],
               bass.AP(b["onesb"], 0, [[128, 1], [0, 256]]),
               start=True, stop=True, inc=s["GD"])
            s["GD"].bump()
            p.emit("vector", lambda e, t=s["GD"].v: e.wait_ge(s["GD"].h, t))
            p.emit("vector", lambda e: e.tensor_copy(offs[:], psT[0:8, 0:1]))
            emit_fence(p, "vector", fbuf, s["KP"])
            p.emit("gpsimd", lambda e, t=s["KP"].v: e.wait_ge(s["KP"].h, t))
            p.emit("gpsimd", lambda e: e.dma_start(
                out=offs_dram.ap(), in_=offs[:]).then_inc(s["CCD"].h, 16))
            s["CCD"].bump(16)
            p.emit("gpsimd", lambda e, t=s["CCD"].v: e.wait_ge(s["CCD"].h, t))
            p.emit("gpsimd", lambda e: e.dma_start(
                out=offs2[:], in_=offs_dram.ap()).then_inc(s["CCD"].h, 16))
            s["CCD"].bump(16)
            p.emit("vector", lambda e, t=s["CCD"].v: e.wait_ge(s["CCD"].h, t))
            p.emit("vector", lambda e: e.tensor_scalar(
                loc[0:8, :], loc[0:8, :], offs2[:], CAP + 0.5, ALU.add, ALU.is_le))
            p.emit("vector", lambda e: e.tensor_tensor(
                keepb[0:8, :], loc[0:8, :], Mask32[0:8, :], ALU.mult).then_inc(
                s["PH2"].h, 1))
            s["PH2"].bump()

            if dbg:
                p.emit("sync", lambda e, t=s["PH2"].v: e.wait_ge(s["PH2"].h, t))
                p.emit("sync", lambda e: e.dma_start(
                    out=dbg_lg, in_=lgall[:]).then_inc(s["ST"].h, 16))
                s["ST"].bump(16)
                p.emit("sync", lambda e: e.dma_start(
                    out=dbg_lg2, in_=lgall2[:]).then_inc(s["ST"].h, 16))
                s["ST"].bump(16)
                p.emit("sync", lambda e: e.dma_start(
                    out=dbg_gm, in_=gmall[:]).then_inc(s["ST"].h, 16))
                s["ST"].bump(16)
                p.emit("sync", lambda e: e.dma_start(
                    out=dbg_z, in_=zall[:]).then_inc(s["ST"].h, 16))
                s["ST"].bump(16)
                p.emit("sync", lambda e: e.dma_start(
                    out=dbg_nm, in_=nmall[:]).then_inc(s["ST"].h, 16))
                s["ST"].bump(16)
                p.emit("sync", lambda e: e.dma_start(
                    out=dbg_gate, in_=gate_row[:]).then_inc(s["ST"].h, 16))
                s["ST"].bump(16)
                p.emit("sync", lambda e: e.dma_start(
                    out=dbg_mask, in_=Mask32[:]).then_inc(s["ST"].h, 16))
                s["ST"].bump(16)
                p.emit("gpsimd", lambda e: e.dma_start(
                    out=dbg_keep, in_=keepb[:]).then_inc(s["CCD"].h, 16))
                s["CCD"].bump(16)
                p.emit("gpsimd", lambda e: e.dma_start(
                    out=dbg_wg, in_=wgsb[:]).then_inc(s["CCD"].h, 16))
                s["CCD"].bump(16)
                p.emit("gpsimd", lambda e: e.dma_start(
                    out=dbg_xn, in_=xnf[:]).then_inc(s["CCD"].h, 16))
                s["CCD"].bump(16)
                p.emit("sync", lambda e: e.dma_start(
                    out=dbg_call, in_=call[:]).then_inc(s["ST"].h, 16))
                s["ST"].bump(16)
            stv_all = s["STV"].v
            xn_all = s["XN"].v
            for i in range(nt):
                par = i % 2
                col = i * TT
                if i == 0:
                    p.emit("sync", lambda e, t=xn_all: e.wait_ge(s["XN"].h, t))
                p.emit("sync", lambda e, t=s["ST"].v: e.wait_ge(s["ST"].h, t))
                p.emit("sync", lambda e, par=par, i=i: e.dma_start(
                    out=b["xbuf"][:, par, :, :],
                    in_=src_x[:, :, i * TT:(i + 1) * TT]).then_inc(s["X"].h, 16))
                s["X"].bump(16)
                x_t = s["X"].v
                p.emit("tensor", lambda e, t=s["PH2"].v: e.wait_ge(s["PH2"].h, t))
                if i == 0:
                    p.emit("tensor", lambda e, t=stv_all: e.wait_ge(s["STV"].h, t))
                else:
                    p.emit("tensor", lambda e, t=s["EPK"].v: e.wait_ge(s["EPK"].h, t))
                p.emit("tensor", lambda e, col=col: e.matmul(
                    ps[0][0:1, :], b["onesb"][0:32, 0:1], keepb[:, col:col + TT],
                    start=True, stop=True).then_inc(s["GD"].h, 1))
                s["GD"].bump()
                p.emit("vector", lambda e, t=s["GD"].v: e.wait_ge(s["GD"].h, t))
                p.emit("vector", lambda e, col=col: e.scalar_tensor_tensor(
                    grow32[0:1, :], ps[0][0:1, :], 0.5, gate_row[0:1, col:col + TT],
                    ALU.mult, ALU.mult).then_inc(s["EPK"].h, 1))
                s["EPK"].bump()
                p.emit("tensor", lambda e, t=s["EPK"].v: e.wait_ge(s["EPK"].h, t))
                if i == 0:
                    p.emit("tensor", lambda e, t=xn_all: e.wait_ge(s["XN"].h, t))
                else:
                    p.emit("tensor", lambda e, t=s["EPG"].v: e.wait_ge(s["EPG"].h, t))
                p.emit("tensor", lambda e: e.matmul(
                    ps[2][:], b["onesb"][0:32, :], grow32[:],
                    start=True, stop=True).then_inc(s["GD"].h, 1))
                s["GD"].bump()
                gb_t = s["GD"].v
                for ot in range(DC):
                    bank = ot % 2
                    if epe_uses[bank] > 0:
                        p.emit("tensor", lambda e, bk=bank, t=EPE[bank].v:
                               e.wait_ge(EPE[bk].h, t))
                    for kc in range(DC):
                        mm(psE[bank][:], wesb[:, ot, kc * 128:(kc + 1) * 128],
                           xnf[:, kc, col:col + TT], start=(kc == 0),
                           stop=(kc == DC - 1), inc=s["GD"] if kc == DC - 1 else None)
                    s["GD"].bump()
                    p.emit("scalar", lambda e, t=s["GD"].v: e.wait_ge(s["GD"].h, t))
                    if i * DC + ot >= 1:
                        p.emit("scalar", lambda e, t=s["RES"].v: e.wait_ge(s["RES"].h, t))
                    p.emit("scalar", lambda e, ot=ot, bank=bank: e.activation(
                        esb[:], psE[bank][:], AF.Identity,
                        bias=besb[:, ot:ot + 1]).then_inc(EPE[bank].h, 1))
                    EPE[bank].bump()
                    epe_uses[bank] += 1
                    p.emit("vector", lambda e, bk=bank, t=EPE[bank].v:
                           e.wait_ge(EPE[bk].h, t))
                    p.emit("vector", lambda e, t=gb_t: e.wait_ge(s["GD"].h, t))
                    p.emit("vector", lambda e: e.tensor_tensor(
                        t2b[:], esb[:], ps[2][:], ALU.mult))
                    p.emit("vector", lambda e, t=x_t: e.wait_ge(s["X"].h, t))
                    p.emit("vector", lambda e, ot=ot, par=par: e.tensor_tensor(
                        b["xbuf"][:, par, ot, :], b["xbuf"][:, par, ot, :],
                        t2b[:], ALU.add).then_inc(s["RES"].h, 1))
                    s["RES"].bump()
                p.emit("vector", lambda e: e.tensor_copy(
                    t2b[0:1, 0:1], t2b[0:1, 0:1]).then_inc(s["EPG"].h, 1))
                s["EPG"].bump()
                p.emit("sync", lambda e, t=s["RES"].v: e.wait_ge(s["RES"].h, t))
                p.emit("sync", lambda e, par=par, i=i: e.dma_start(
                    out=dst_x[:, :, i * TT:(i + 1) * TT],
                    in_=b["xbuf"][:, par, :, :]).then_inc(s["ST"].h, 16))
                s["ST"].bump(16)
            if dbg2:
                p.emit("gpsimd", lambda e, t=s["ST"].v: e.wait_ge(s["ST"].h, t))
                for dst, srct in ((d2_offs, offs2[:]), (d2_keep, keepb[:]),
                                  (d2_gate, gate_row[:]), (d2_mask, Mask32[:]),
                                  (d2_loc, loc[:])):
                    p.emit("gpsimd", lambda e, d=dst, sr=srct: e.dma_start(
                        out=d, in_=sr).then_inc(s["CCD"].h, 16))
                    s["CCD"].bump(16)
            emit_dma_drain(p, s)
            if s["CCD"].v > 0:
                p.emit("gpsimd", lambda e, t=s["CCD"].v: e.wait_ge(s["CCD"].h, t))
            p.play(block)
    return nc


_PROGS = {}


def _prog(name, builder, *args, **kw):
    if name not in _PROGS:
        _PROGS[name] = builder(*args, **kw)
    return _PROGS[name]


def _sbig_host():
    p = np.arange(128)[:, None]
    idx = np.arange(896)[None, :]
    return ((idx - 384) >= p).astype(np.float32)


def kernel(**inputs):
    g = inputs
    ones = np.ones((128, 128), dtype=np.float32)
    ident = np.eye(128, dtype=np.float32)
    sbig = _sbig_host()

    x0 = np.asarray(g["L0_attn_x"], dtype=np.float32)
    xs = [np.ascontiguousarray(x0[2 * c:2 * c + 2].reshape(TCORE, D).T)
          for c in range(NCORES)]

    T = lambda a: np.ascontiguousarray(np.asarray(a, dtype=np.float32).T)
    tern = {}
    for L in (1, 3):
        tern[L] = {
            "w1s": slab_major(T(g[f"L{L}_ffn_w1_w"])),
            "w2s": slab_major(T(g[f"L{L}_ffn_w2_w"])),
            "h1s": slab_major(T(g[f"L{L}_ffn_h1_w"])),
            "h2s": slab_major(T(g[f"L{L}_ffn_h2_w"])),
            "w3s": slab_major(T(g[f"L{L}_ffn_w3_w"])),
            "s1v": col_major_vec(np.asarray(g[f"L{L}_ffn_w1_s"], dtype=np.float32)),
            "s2v": col_major_vec(np.asarray(g[f"L{L}_ffn_w2_s"], dtype=np.float32)),
            "sh1v": col_major_vec(np.asarray(g[f"L{L}_ffn_h1_s"], dtype=np.float32)),
            "sh2v": col_major_vec(np.asarray(g[f"L{L}_ffn_h2_s"], dtype=np.float32)),
            "s3v": col_major_vec(np.asarray(g[f"L{L}_ffn_w3_s"], dtype=np.float32)),
            "ones_in": ones,
        }
    l1a = {
        "w1s": slab_major(T(g["L1_attn_W1"])),
        "w2s": slab_major(T(g["L1_attn_W2"]), tile_o=16)[0],
        "w3s": slab_major(T(g["L1_attn_W3"]), tile_o=16)[0],
        "b1v": col_major_vec(np.asarray(g["L1_attn_b1"], dtype=np.float32)),
        "b2v": np.ascontiguousarray(
            np.asarray(g["L1_attn_b2"], dtype=np.float32).reshape(16, 1)),
        "b3v": np.ascontiguousarray(
            np.asarray(g["L1_attn_b3"], dtype=np.float32).reshape(16, 1)),
        "ones_in": ones,
    }
    l0f = {
        "wg": np.ascontiguousarray(np.asarray(g["L0_ffn_w_gate"], dtype=np.float32)),
        "wu": np.ascontiguousarray(np.asarray(g["L0_ffn_w_up"], dtype=np.float32)),
        "wd": np.ascontiguousarray(np.asarray(g["L0_ffn_w_down"], dtype=np.float32)),
        "ones_in": ones, "ident_in": ident,
    }
    wqkv = np.asarray(g["L0_attn_W_qkv"], dtype=np.float32)
    l0a = {
        "wqh": slab_major(T(wqkv[0:1024]), tile_o=64),
        "wkh": slab_major(T(wqkv[1024:2048]), tile_o=64),
        "wvh": slab_major(T(wqkv[2048:3072]), tile_o=64),
        "sbig": sbig, "ones_in": ones, "ident_in": ident,
    }
    attn = {}
    for L, qn, kn, vn, on in ((2, "q_heads", "k_heads", "v_heads", "output_proj"),
                              (3, "Wq", "Wk", "Wv", "Wo")):
        attn[L] = {
            "wqh": slab_major(T(g[f"L{L}_attn_{qn}"]), tile_o=64),
            "wkh": slab_major(T(g[f"L{L}_attn_{kn}"]), tile_o=64),
            "wvh": slab_major(T(g[f"L{L}_attn_{vn}"]), tile_o=64),
            "wos": slab_major(T(g[f"L{L}_attn_{on}"])),
            "sbig": sbig, "ones_in": ones, "ident_in": ident,
        }
    we = np.asarray(g["L2_ffn_w_expert"], dtype=np.float32)
    wg_ = np.asarray(g["L2_ffn_w_gate"], dtype=np.float32)
    moe_core = []
    for c in range(NCORES):
        rowsel = np.zeros((1, 8), dtype=np.float32); rowsel[0, c] = 1.0
        prevm = np.zeros((32, 1), dtype=np.float32); prevm[:c, 0] = 1.0
        moe_core.append({
            "wes": slab_major(np.ascontiguousarray(we)),
            "wgr": np.ascontiguousarray(wg_.reshape(8, 128, 8).transpose(1, 0, 2)),
            "bev": col_major_vec(np.asarray(g["L2_ffn_b_expert"], dtype=np.float32)),
            "rowsel": rowsel, "prevm": prevm,
            "ones_in": ones, "ident_in": ident,
        })

    p_tern = _prog("tern", build_ternary)
    p_l1a = _prog("l1a", build_l1attn)
    p_l0f = _prog("l0f", build_l0ffn)
    p_l0a = _prog("l0a", build_attn, causal=False, out_proj=False)
    p_att = _prog("att", build_attn, causal=True, out_proj=True)
    p_moe = _prog("moe", build_moe)

    def launch(prog, extra):
        nonlocal xs
        ims = []
        for c in range(NCORES):
            im = {"x": xs[c]}
            im.update(extra if isinstance(extra, dict) else extra[c])
            ims.append(im)
        res = run_bass_kernel_spmd(prog, ims, core_ids=list(range(NCORES)))
        xs = [np.ascontiguousarray(res.results[c]["xo"]) for c in range(NCORES)]

    seq = [
        (p_l0f, l0f), (p_l0a, l0a), (p_l0f, l0f),
        (p_tern, tern[1]), (p_l1a, l1a), (p_tern, tern[1]),
        (p_moe, moe_core), (p_att, attn[2]), (p_moe, moe_core),
        (p_tern, tern[3]), (p_att, attn[3]), (p_tern, tern[3]),
    ]
    for prog, extra in seq:
        launch(prog, extra)

    out = np.empty((16, 2048, D), dtype=np.float32)
    for c in range(NCORES):
        out[2 * c:2 * c + 2] = xs[c].T.reshape(2, 2048, D)
    return out
